# version 07
# baseline (speedup 1.0000x reference)
import numpy as np
import ml_dtypes
from contextlib import ExitStack

import concourse.bass as bass
import concourse.mybir as mybir
from concourse.bass_utils import run_bass_kernel_spmd

F32 = mybir.dt.float32
BF16 = mybir.dt.bfloat16
I32 = mybir.dt.int32
AF = mybir.ActivationFunctionType
Alu = mybir.AluOpType
AX = mybir.AxisListType

D = 2048
NH = 16
DH = 128
RQ, RK, RV = 6, 2, 2
NB = RQ + RK + RV
IFFN = 8192
GN = 64
MSTAB = 30.0
INVSQ = float(np.float32(1.0 / np.sqrt(128.0)))
FWHT_NORM = float(np.float32(1.0 / np.sqrt(8192.0)))
EPS_LN = 1e-5
EPS_Q = 1e-5
F32TINY = float(np.finfo(np.float32).tiny)
LN2 = 0.6931471805599453
LN2_HI = float(np.float32(LN2))
LN2_LO2 = float(np.float32(LN2 - np.float64(np.float32(LN2))))
IL2 = 1.4426950408889634
IL2_HI = float(np.float32(IL2))
IL2_LO = float(np.float32(IL2 - np.float64(np.float32(IL2))))
NUDGE = float(np.float32(2.0 ** -38))
INV2_24 = float(np.float32(2.0 ** -24))
DEBUG_TAPS = False
DEBUG_DUMP = False


class KB:
    ENGS = ("sync", "scalar", "vector", "tensor", "gpsimd")

    def __init__(self, nc):
        self.nc = nc
        self.ops = {e: [] for e in self.ENGS}
        self.semh = {}
        self.cnt = {}
        self.phase = "P0"

    def sem(self, name, stack):
        self.semh[name] = stack.enter_context(self.nc.semaphore(name))
        self.cnt[name] = 0

    def emit(self, eng, fn, waits=(), inc=None, inc_by=1, is_dma=False):
        semh = self.semh
        if eng in ("vector", "scalar") and not is_dma:
            sem = "VE" if eng == "vector" else "SE"
            waits = list(waits) + [(sem, self.cnt[sem])]
            inc, inc_by = sem, 1
        if inc is not None:
            self.cnt[inc] += inc_by
        waits = tuple((s, c) for (s, c) in waits if c > 0)

        def run(e, fn=fn, waits=waits, inc=inc, inc_by=inc_by):
            for s, c in waits:
                e.wait_ge(semh[s], c)
            ins = fn(e)
            if inc is not None:
                ins.then_inc(semh[inc], inc_by)
        self.ops[eng].append((self.phase, run))

    def dma(self, eng, dst, src, waits=(), sem=None, slow=False):
        sem = sem or {"gpsimd": "GD", "scalar": "SD"}.get(eng, "LD")
        waits = list(waits) + [(sem, self.cnt[sem])]
        self.emit(eng, lambda e, d=dst, s=src, sl=slow: e.dma_start(
            out=d, in_=s, allow_slow_non_contiguous=sl),
            waits=waits, inc=sem, inc_by=16, is_dma=True)
        return self.cnt[sem]

    def wait(self, eng, sem, val):
        semh = self.semh
        if val > 0:
            self.ops[eng].append(
                (self.phase, lambda e, s=sem, v=val: e.wait_ge(semh[s], v)))

    def c(self, s):
        return self.cnt[s]

    def barrier(self):
        snap = {s: self.cnt[s] for s in ("LD", "SD", "GD", "CC", "PE", "VE", "SE")}
        for eng in self.ENGS:
            for s, v in snap.items():
                self.wait(eng, s, v)



def _finish(nc, kb, top, close=True):
    def replay(e, lst):
        cur, scope = None, None
        for ph, f in lst:
            if ph != cur:
                if scope is not None:
                    nc.leave_named_scope(cur, scope, False)
                cur = ph
                scope, _ = nc.enter_named_scope(cur, False)
            f(e)
        if scope is not None:
            nc.leave_named_scope(cur, scope, False)

    with nc.Block() as blk:
        @blk.sync
        def _(e):
            replay(e, kb.ops["sync"])

        @blk.scalar
        def _(e):
            replay(e, kb.ops["scalar"])

        @blk.vector
        def _(e):
            replay(e, kb.ops["vector"])

        @blk.tensor
        def _(e):
            replay(e, kb.ops["tensor"])

        @blk.gpsimd
        def _(e):
            replay(e, kb.ops["gpsimd"])

    if close:
        top.close()
    return nc


class _Trunc(Exception):
    pass


def build_program(n_cores, TC, upto=99):
    try:
        return _build_program(n_cores, TC, upto)
    except _Trunc as t:
        c = t.args[0]
        kb, nc, top = c["kb"], c["nc"], c["top"]
        kb.dma("sync", c["out"][:, :], c["x_in"][:, :], sem="OD",
               waits=[("SE", kb.c("SE")), ("VE", kb.c("VE")),
                      ("PE", kb.c("PE")), ("GD", kb.c("GD")),
                      ("CC", kb.c("CC")), ("LD", kb.c("LD"))])
        kb.wait("sync", "OD", kb.c("OD"))
        if DEBUG_DUMP:
            for nm, tt in c["dumps"]:
                dbg = nc.dram_tensor(nm, list(tt.tensor.shape), tt.tensor.dtype,
                                     kind="ExternalOutput").ap()
                ix = tuple([slice(None)] * len(tt.tensor.shape))
                kb.dma("sync", dbg[ix], tt[ix], sem="OD")
            kb.wait("sync", "OD", kb.c("OD"))
        return _finish(nc, kb, top, close=False)


def _build_program(n_cores, TC, upto=99):
    T = n_cores * TC
    HPC = NH // n_cores
    NA = (RQ + RK + RV) * HPC
    NAall = NA * n_cores
    CSL = IFFN // n_cores
    OSL = D // n_cores
    TCH = TC // 128
    QB = 512
    NQB = T // QB
    KD = D // 128
    KI = IFFN // 128
    NO = D // 128

    SEOFF = 1 if (TC // 512) >= 2 else 0
    nc = bass.Bass("TRN2", target_bir_lowering=False,
                   detect_race_conditions=False)
    kb = KB(nc)
    top = ExitStack()

    def din(name, shape, dt=F32):
        return nc.dram_tensor(name, list(shape), dt, kind="ExternalInput").ap()

    def dint(name, shape, dt=F32):
        return nc.dram_tensor(name, list(shape), dt, kind="Internal").ap()

    x_in = din("x_slice", [TC, D])
    waT = din("waT", [D, NAall])
    wbT = din("wbT", [D, NB * DH])
    woT = din("woT", [D, D])
    wgTs = din("wgTs", [D, CSL])
    wuTs = din("wuTs", [D, CSL])
    wdTs = din("wdTs", [IFFN, OSL])
    l1w_s = din("l1w_s", [OSL, D])
    l1wvT = din("l1wvT", [D, DH])
    l1bv = din("l1bv", [DH, 1])
    l1bias = din("l1bias", [D])
    identf = din("identf", [128, 128])
    identb = din("identb", [128, 128], BF16)
    h128 = din("h128", [128, 128])
    h64 = din("h64", [64, 64])
    masks = din("masks", [128, 4 * QB])
    ctab_in = din("ctab", [128, 2 * TC])
    stab_in = din("stab", [128, 2 * TC])
    rank_in = din("rank_id", [1, 1], I32)
    out = nc.dram_tensor("out", [TC, D], F32, kind="ExternalOutput").ap()

    ag1_in = dint("ag1_in", [TC, NB * DH])
    ag1_out = dint("ag1_out", [T, NB * DH])
    ag1b_in = dint("ag1b_in", [NAall, TC])
    ag1b_out_h = nc.dram_tensor("ag1b_out", [n_cores * NAall, TC], F32,
                                kind="Internal")
    ag1b_out = ag1b_out_h.ap()
    ag2_in = dint("ag2_in", [HPC * DH, T])
    ag2_out_h = nc.dram_tensor("ag2_out", [n_cores * HPC * DH, T], F32,
                               kind="Internal")
    ag2_out = ag2_out_h.ap()
    a_mine_dram = dint("a_mine_dram", [NA, T])
    y_sel_dram = dint("y_sel_dram", [D, TC])
    ar_in = dint("ar_in", [4, 1])
    ar_out = dint("ar_out", [4, 1])
    wgq_in = dint("wgq_in", [D, CSL], BF16)
    wgq_out = dint("wgq_out", [n_cores, D, CSL], BF16)
    wuq_in = dint("wuq_in", [D, CSL], BF16)
    wuq_out = dint("wuq_out", [n_cores, D, CSL], BF16)
    wdq_in = dint("wdq_in", [IFFN, OSL], BF16)
    wdq_out = dint("wdq_out", [n_cores, IFFN, OSL], BF16)
    l1q_in = dint("l1q_in", [D, OSL], BF16)
    l1q_out = dint("l1q_out", [n_cores, D, OSL], BF16)
    xnT_dram = dint("xnT_dram", [D, TC])
    xnqT_dram = dint("xnqT_dram", [D, TC], BF16)
    u_dram = dint("u_dram", [GN, 128, TC])
    s_dram = dint("s_dram", [D, TC])
    ft_dram = dint("ft_dram", [TC])
    fz_dram = dint("fz_dram", [TC])
    xnew_dram = dint("xnew_dram", [TC, D])
    if DEBUG_TAPS:
        tap_qt = dint("tap_qt", [128, T])
        tap_kt = dint("tap_kt", [128, T])
        tap_v = dint("tap_v", [128, T])
        tap_g = dint("tap_g", [128, TC])
        tap_y = dint("tap_y", [128, TC])
    xn1T_dram = dint("xn1T_dram", [D, TC])
    aqT_dram = dint("aqT_dram", [D, TC], BF16)

    for s in ("LD", "SD", "GD", "CC", "PE", "VE", "SE", "OD"):
        kb.sem(s, top)

    sbuf = lambda n, sh, dt=F32: top.enter_context(nc.sbuf_tensor(n, list(sh), dt))
    cpool = sbuf("cpool", [128, 24])
    identf_t = sbuf("identf_t", [128, 128])
    identb_t = sbuf("identb_t", [128, 128], BF16)
    h128_t = sbuf("h128_t", [128, 128])
    h64_t = sbuf("h64_t", [64, 64])
    masks_t = sbuf("masks_t", [128, 4 * QB])
    ones_c = sbuf("ones_c", [128, 1])
    zero_c = sbuf("zero_c", [128, 1])
    ones_r = sbuf("ones_r", [1, 128])
    l1bv_t = sbuf("l1bv_t", [DH, 1])
    l1bias_t = sbuf("l1bias_t", [128, NO])
    sc_t = sbuf("sc_t", [128, 8])
    bst = sbuf("bst", [128, 24])
    mv_t = sbuf("mv_t", [128, 2])
    row1 = sbuf("row1", [128, 8])
    row2 = sbuf("row2", [128, 8])
    evb0 = sbuf("evb0", [128, 512])
    evb1 = sbuf("evb1", [128, 512])
    evb2 = sbuf("evb2", [128, 512])
    evb3 = sbuf("evb3", [128, 512])
    rowq = sbuf("rowq", [1, QB])
    srk = sbuf("srk", [1, 1], I32)
    SCR = {}

    def alloc_scr(stack, tag):
        for nm, sh, dt_ in (("Xs", [128, 2048], F32), ("Qs", [128, 2048], F32),
                            ("TAs", [128, 2048], F32), ("TBs", [128, 2048], F32),
                            ("QIs", [128, 2048], I32), ("SAs", [128, 64], F32),
                            ("SBs", [128, 64], F32), ("SCs", [128, 64], F32),
                            ("SDs", [128, 64], F32)):
            SCR[nm] = stack.enter_context(
                nc.sbuf_tensor(nm + tag, list(sh), dt_))

    PS = [top.enter_context(nc.psum_tensor(f"ps{i}", [128, 512], F32)) for i in range(8)]

    cvals = []
    def C(v):
        v32 = float(np.float32(v))
        if v32 not in cvals:
            cvals.append(v32)
        return v32
    for v in (-MSTAB, INVSQ, EPS_LN, LN2_HI, LN2_LO2, 1.0):
        C(v)
    assert len(cvals) <= 24
    for i, v in enumerate(cvals):
        nc.const_aps.register(F32, v, cpool[:, i:i + 1])
    for i, v in enumerate(cvals):
        kb.emit("vector", lambda e, v=v, i=i: e.memset(cpool[:, i:i + 1], v))
    kb.emit("vector", lambda e: e.memset(ones_c[:], 1.0))
    kb.emit("vector", lambda e: e.memset(zero_c[:], 0.0))
    kb.emit("vector", lambda e: e.memset(ones_r[:], 1.0), inc="VE")
    VE_CONST = kb.c("VE")

    def mm(ps_ap, lhs, rhs, start, stop, waits=(), inc=None):
        kb.emit("tensor", lambda e, o=ps_ap, l=lhs, r=rhs, a=start, b=stop:
                e.matmul(o, l, r, start=a, stop=b), waits=waits, inc=inc)

    def tr(ps_ap, in_ap, bf=False, waits=(), inc="PE"):
        idt = identb_t if bf else identf_t
        p = in_ap.partition_size()
        kb.emit("tensor", lambda e, o=ps_ap, i=in_ap, d=idt, p=p:
                e.transpose(o, i, d[0:p, 0:p]), waits=waits, inc=inc)

    def act(dst, src, func, bias, scale, waits=()):
        kb.emit("scalar", lambda e, d=dst, s=src, f=func, b=bias, sc=scale:
                e.activation(d, s, f, bias=b, scale=sc), waits=waits, inc="SE")

    def vcopy(dst, src, waits=(), inc=None):
        kb.emit("vector", lambda e, d=dst, s=src: e.tensor_copy(d, s),
                waits=waits, inc=("VE" if inc else None))

    def vtt(op, dst, a, b, waits=(), inc=None):
        kb.emit("vector", lambda e, d=dst, x=a, y=b, o=op: e.tensor_tensor(d, x, y, o),
                waits=waits, inc=("VE" if inc else None))

    def vts(dst, a, s1, s2, op1, op2=None, waits=(), inc=None):
        if s2 is None:
            kb.emit("vector", lambda e, d=dst, x=a, s=s1, o=op1:
                    e.tensor_single_scalar(d, x, s, o), waits=waits,
                    inc=("VE" if inc else None))
        else:
            kb.emit("vector", lambda e, d=dst, x=a, u=s1, v=s2, o1=op1, o2=op2:
                    e.tensor_scalar(d, x, u, v, o1, o2), waits=waits,
                    inc=("VE" if inc else None))

    def vstt(dst, in0, scalar, in1, op1, op2, waits=(), inc=None):
        kb.emit("vector", lambda e, d=dst, a=in0, s=scalar, b=in1, o1=op1, o2=op2:
                e.scalar_tensor_tensor(d, a, s, b, o1, o2), waits=waits,
                inc=("VE" if inc else None))

    def vrecip(dst, src, waits=(), inc=None):
        kb.emit("vector", lambda e, d=dst, s=src: e.reciprocal(d, s),
                waits=waits, inc=("VE" if inc else None))

    def emit_round(t_ap, i_ap, r_ap, g_ap, waits=()):
        vcopy(i_ap, t_ap, waits=waits)
        vcopy(r_ap, i_ap)
        vtt(Alu.subtract, g_ap, t_ap, r_ap)
        vts(t_ap, g_ap, 0.5, None, Alu.is_gt)
        vts(g_ap, g_ap, -0.5, None, Alu.is_lt)
        vtt(Alu.subtract, t_ap, t_ap, g_ap)
        vtt(Alu.add, t_ap, t_ap, r_ap)


    _ctx = {"nc": nc, "kb": kb, "top": top, "out": out, "x_in": x_in,
            "stacks": [],
            "dumps": [("d_xnT", xnT_dram), ("d_ft", ft_dram),
                      ("d_xnqT", xnqT_dram)]}

    kb.dma("gpsimd", srk[:], rank_in[:])
    GD_RANK = kb.c("GD")
    kb.dma("sync", identf_t[:], identf[:])
    kb.dma("sync", identb_t[:], identb[:])
    kb.dma("sync", h128_t[:], h128[:])
    kb.dma("sync", h64_t[:], h64[:])
    kb.dma("sync", masks_t[:], masks[:])
    kb.dma("sync", l1bv_t[:], l1bv[:])
    kb.dma("sync", l1bias_t[:], l1bias.rearrange("(a b) -> b a", b=128), slow=True)

    scrA = ExitStack()
    _ctx["stacks"].append(scrA)
    alloc_scr(scrA, "_a")
    with ExitStack() as pA:
        ctab_t = pA.enter_context(nc.sbuf_tensor("ctab_t", [128, TC], F32))
        stab_t = pA.enter_context(nc.sbuf_tensor("stab_t", [128, TC], F32))
        kb.dma("sync", ctab_t[:], ctab_in[:, 0:TC])
        kb.dma("sync", stab_t[:], stab_in[:, 0:TC])
        for src_t, dst_t in ((ctab_in, ctab_t), (stab_in, stab_t)):
            for i in range(TC // 512):
                ld = kb.dma("sync", evb0[:, 0:512],
                            src_t[:, TC + i * 512:TC + (i + 1) * 512],
                            waits=[("VE", kb.c("VE"))])
                kb.emit("vector", lambda e, i=i, d=dst_t: e.tensor_add(
                    d[:, i * 512:(i + 1) * 512], d[:, i * 512:(i + 1) * 512],
                    evb0[:, 0:512]), waits=[("LD", ld)], inc="VE")
        VE_TABS = kb.c("VE")

        kb.phase = "P1"
        with ExitStack() as p1:
            xtok = p1.enter_context(nc.sbuf_tensor("xtok", [128, TCH * D], F32))
            xnq = p1.enter_context(nc.sbuf_tensor("xnq", [128, TCH * D], BF16))
            asm = p1.enter_context(nc.sbuf_tensor("asm", [128, TC], F32))
            asmb = p1.enter_context(nc.sbuf_tensor("asmb", [128, TC], BF16))

            for i in range(TCH):
                kb.dma("sync", xtok[:, i * D:(i + 1) * D],
                       x_in[i * 128:(i + 1) * 128, :])
            LD_X = kb.c("LD")

            for i in range(TCH):
                xc = xtok[:, i * D:(i + 1) * D]
                for sg in range(4):
                    kb.emit("vector", lambda e, x=xc, sg=sg: e.bn_stats(
                        bst[:, sg * 6:(sg + 1) * 6], x[:, sg * 512:(sg + 1) * 512]),
                        waits=([("LD", LD_X), ("VE", VE_CONST),
                                ("SE", kb.c("SE"))] if sg == 0 else []))
                kb.emit("vector", lambda e: e.bn_aggr(mv_t[:], bst[:]), inc="VE")
                act(row1[:, 0:1], mv_t[:, 1:2], AF.Sqrt, C(EPS_LN), 1.0,
                    waits=[("VE", kb.c("VE"))])
                vrecip(row1[:, 1:2], row1[:, 0:1], waits=[("SE", kb.c("SE"))])
                kb.emit("vector", lambda e, x=xc: e.scalar_tensor_tensor(
                    x.rearrange("p (a b) -> p a b", b=D),
                    x.rearrange("p (a b) -> p a b", b=D),
                    mv_t[:, 0:1],
                    row1[:, 1:2].unsqueeze(2).broadcast_to([128, 1, D]),
                    Alu.subtract, Alu.mult), inc="VE")
                kb.emit("vector", lambda e, x=xc: e.reduce_max(
                    row1[:, 2:3], x, AX.X, apply_absolute_value=True))
                vts(row1[:, 3:4], row1[:, 2:3], EPS_Q, None, Alu.max)
                vrecip(row1[:, 4:5], row1[:, 3:4])
                vts(row1[:, 5:6], row1[:, 4:5], 128.0, None, Alu.mult)
                vts(row2[:, i:i + 1], row1[:, 3:4], 0.0078125, None, Alu.mult,
                    inc=True)
                act(SCR["TAs"][:], xc, AF.Identity, zero_c[:, 0:1], row1[:, 5:6],
                    waits=[("VE", kb.c("VE"))])
                emit_round(SCR["TAs"][:], SCR["QIs"][:], SCR["TBs"][:], SCR["Xs"][:], waits=[("SE", kb.c("SE"))])
                vts(xnq[:, i * D:(i + 1) * D], SCR["TAs"][:], 127.0, None, Alu.min, inc=True)
            VE_XQ = kb.c("VE")

            tr(PS[6][0:TCH, 0:128], row2[:, 0:TCH], waits=[("VE", VE_XQ)])
            vcopy(evb1[0:TCH, 0:128], PS[6][0:TCH, 0:128],
                  waits=[("PE", kb.c("PE"))], inc=True)
            kb.dma("sync", ft_dram.rearrange("(a b) -> a b", b=128),
                   evb1[0:TCH, 0:128], waits=[("VE", kb.c("VE"))])

            for i in range(KD):
                for j in range(TCH):
                    pst = PS[4 + (j % 2)]
                    tr(pst[:, 0:128], xtok[:, j * D + i * 128:j * D + (i + 1) * 128],
                       waits=[("VE", VE_XQ), ("SE", kb.c("SE"))])
                    act(asm[:, j * 128:(j + 1) * 128], pst[:, 0:128], AF.Copy,
                        0.0, 1.0, waits=[("PE", kb.c("PE")), ("LD", kb.c("LD"))])
                kb.dma("sync", xnT_dram[i * 128:(i + 1) * 128, 0:TC], asm[:, 0:TC],
                       waits=[("SE", kb.c("SE"))])
            for i in range(KD):
                for j in range(TCH):
                    pst = PS[4 + (j % 2)]
                    tr(pst[:].bitcast(BF16)[:, 0:128],
                       xnq[:, j * D + i * 128:j * D + (i + 1) * 128], bf=True,
                       waits=[("SE", kb.c("SE")), ("LD", kb.c("LD"))])
                    act(asmb[:, j * 128:(j + 1) * 128],
                        pst[:].bitcast(BF16)[:, 0:128], AF.Copy, 0.0, 1.0,
                        waits=[("PE", kb.c("PE")), ("LD", kb.c("LD"))])
                kb.dma("sync", xnqT_dram[i * 128:(i + 1) * 128, 0:TC], asmb[:, 0:TC],
                       waits=[("SE", kb.c("SE"))])
        kb.barrier()
        if upto <= 1:
            raise _Trunc(_ctx)

        kb.phase = "P2"
        with ExitStack() as p2:
            waT_t = p2.enter_context(nc.sbuf_tensor("waT_t", [128, KD * NAall], F32))
            aout = p2.enter_context(nc.sbuf_tensor("aout", [128, 2 * TC], F32))
            xnf = p2.enter_context(nc.sbuf_tensor("xnf", [128, KD * TC], F32))
            wb0 = p2.enter_context(nc.sbuf_tensor("wb0", [128, KD * 128], F32))
            wb1 = p2.enter_context(nc.sbuf_tensor("wb1", [128, KD * 128], F32))
            bscr = p2.enter_context(nc.sbuf_tensor("bscr", [128, TC], F32))
            byy = p2.enter_context(nc.sbuf_tensor("byy", [128, TC], F32))
            xs_t = p2.enter_context(nc.sbuf_tensor("xs_t", [128, TC], F32))

            kb.dma("gpsimd", waT_t[:].rearrange("p (a c) -> p a c", c=NAall),
                   waT[:, :].rearrange("(a p) c -> p a c", p=128))
            SD_WA = kb.c("GD")
            for k in range(KD):
                ldxn = kb.dma("sync", xnf[:, k * TC:(k + 1) * TC],
                              xnT_dram[k * 128:(k + 1) * 128, :])

            na_parts = [(0, min(128, NAall))]
            if NAall > 128:
                na_parts.append((128, NAall - 128))
            for (o0, osz) in na_parts:
                for th in range(TC // 512):
                    ps = PS[0 + (th % 2)]
                    se_guard = kb.c("SE")
                    for k in range(KD):
                        mm(ps[0:osz, 0:512],
                           waT_t[:, k * NAall + o0:k * NAall + o0 + osz],
                           xnf[:, k * TC + th * 512:k * TC + (th + 1) * 512],
                           start=(k == 0), stop=(k == KD - 1),
                           waits=([("LD", ldxn), ("GD", SD_WA),
                                   ("SE", max(0, se_guard - SEOFF))]
                                  if k == 0 else []),
                           inc="PE")
                    coff = (0 if o0 == 0 else TC) + th * 512
                    act(aout[0:osz, coff:coff + 512], ps[0:osz, 0:512],
                        AF.Copy, 0.0, 1.0, waits=[("PE", kb.c("PE"))])
            kb.dma("sync", ag1b_in[0:128, :], aout[:, 0:TC],
                   waits=[("SE", kb.c("SE"))])
            if NAall > 128:
                kb.dma("sync", ag1b_in[128:NAall, :], aout[0:NAall - 128, TC:2 * TC],
                       waits=[("SE", kb.c("SE"))])
            LD_A2A1 = kb.c("LD")

            wb_guard = {}
            for r in range(NB):
                wbuf = wb0 if r % 2 == 0 else wb1
                ldw = kb.dma("gpsimd", wbuf[:].rearrange("p (a c) -> p a c", c=128),
                             wbT[:, r * 128:(r + 1) * 128].rearrange(
                                 "(a p) c -> p a c", p=128),
                             waits=[("PE", wb_guard.get(r % 2, 0))])
                for th in range(TC // 512):
                    ps = PS[2 + (th % 2)]
                    se_guard = kb.c("SE")
                    for k in range(KD):
                        mm(ps[:, 0:512], wbuf[:, k * 128:(k + 1) * 128],
                           xnf[:, k * TC + th * 512:k * TC + (th + 1) * 512],
                           start=(k == 0), stop=(k == KD - 1),
                           waits=([("LD", ldxn), ("GD", ldw),
                                   ("SE", max(0, se_guard - SEOFF))]
                                  if k == 0 else []),
                           inc="PE")
                    act(bscr[:, th * 512:(th + 1) * 512], ps[:, 0:512],
                        AF.Copy, 0.0, 1.0, waits=[("PE", kb.c("PE"))])
                sev = kb.c("SE")
                if r >= RQ:
                    x1 = bscr[0:64, 0:TC]; x2 = bscr[64:128, 0:TC]
                    y1 = byy[0:64, 0:TC]; y2 = byy[64:128, 0:TC]
                    kb.dma("sync", xs_t[0:64, 0:TC], x2,
                           waits=[("SE", sev), ("VE", kb.c("VE"))])
                    ldsw = kb.dma("sync", xs_t[64:128, 0:TC], x1)
                    w0 = [("SE", sev), ("VE", VE_TABS)]
                    vtt(Alu.mult, y1, x1, ctab_t[0:64, 0:TC], waits=w0)
                    vtt(Alu.mult, y2, x2, ctab_t[64:128, 0:TC])
                    for th in range(TC // 512):
                        sl = slice(th * 512, (th + 1) * 512)
                        vtt(Alu.mult, evb0[0:64, 0:512], xs_t[0:64, sl],
                            stab_t[0:64, sl],
                            waits=[("LD", ldsw)] if th == 0 else ())
                        vtt(Alu.add, y1[:, sl], y1[:, sl], evb0[0:64, 0:512])
                        vtt(Alu.mult, evb0[64:128, 0:512], xs_t[64:128, sl],
                            stab_t[64:128, sl])
                        vtt(Alu.subtract, y2[:, sl], y2[:, sl], evb0[64:128, 0:512])
                    vcopy(x1, y1)
                    vcopy(x2, y2, inc=True)
                vev = kb.c("VE")
                for j in range(TCH):
                    pst = PS[4 + (j % 2)]
                    eb = evb1 if j % 2 == 0 else evb2
                    tr(pst[:, 0:128], bscr[:, j * 128:(j + 1) * 128],
                       waits=[("SE", sev), ("VE", vev), ("LD", kb.c("LD"))])
                    act(eb[:, 0:128], pst[:, 0:128], AF.Copy, 0.0, 1.0,
                        waits=[("PE", kb.c("PE"))])
                    kb.dma("sync",
                           ag1_in[j * 128:(j + 1) * 128, r * 128:(r + 1) * 128],
                           eb[:, 0:128], waits=[("SE", kb.c("SE"))])
                wb_guard[r % 2] = kb.c("PE")
            LD_AG1 = kb.c("LD")

            kb.emit("gpsimd", lambda e: e.collective_compute(
                "AllGather", Alu.bypass, replica_groups=[list(range(n_cores))],
                ins=[ag1_in[:, :].opt()], outs=[ag1_out[:, :].opt()]),
                waits=[("LD", LD_AG1)], inc="CC")
            kb.emit("gpsimd", lambda e: e.collective_compute(
                "AllGather", Alu.bypass, replica_groups=[list(range(n_cores))],
                ins=[ag1b_in[:, :].opt()], outs=[ag1b_out[:, :].opt()]),
                waits=[("LD", LD_A2A1)], inc="CC")
            CC_AG1B = kb.c("CC")
            def sel_a(e):
                with e.register("rka") as rka:
                    e.reg_load(rka, srk[0:1, 0:1])
                    e.reg_mul(rka, rka, NA * TC)
                    srcap = bass.AP(ag1b_out_h, rka,
                                    [[TC, NA], [NAall * TC, n_cores], [1, TC]])
                    return e.dma_start(
                        out=a_mine_dram[:, :].rearrange("r (j c) -> r j c", c=TC),
                        in_=srcap)
            kb.emit("gpsimd", sel_a,
                    waits=[("CC", CC_AG1B), ("GD", kb.c("GD"))],
                    inc="GD", inc_by=16)
        CC_P2 = kb.c("CC")
        GD_AMINE = kb.c("GD")
    kb.barrier()
    if upto <= 2:
        raise _Trunc(_ctx)

    kb.phase = "P3"
    def mxfp4_chunk(src_ap, dstq_ap, waits=()):
        kb.emit("vector", lambda e, s=src_ap, d=SCR["SAs"][:]: e.tensor_reduce(
            d, s.rearrange("p (a b) -> p a b", b=32), AX.X, Alu.max,
            apply_absolute_value=True), waits=waits)
        vts(SCR["SBs"][:], SCR["SAs"][:], F32TINY, None, Alu.max, inc=True)
        act(SCR["SBs"][:], SCR["SBs"][:], AF.Ln, 0.0, 1.0, waits=[("VE", kb.c("VE"))])
        vts(SCR["SCs"][:], SCR["SBs"][:], IL2_HI, NUDGE, Alu.mult, Alu.add,
            waits=[("SE", kb.c("SE"))])
        vstt(SCR["SCs"][:], SCR["SBs"][:], IL2_LO, SCR["SCs"][:], Alu.mult, Alu.add)
        vcopy(SCR["QIs"][:, 0:64], SCR["SCs"][:])
        vcopy(SCR["SBs"][:], SCR["QIs"][:, 0:64])
        vtt(Alu.subtract, SCR["SCs"][:], SCR["SCs"][:], SCR["SBs"][:])
        vts(SCR["SDs"][:], SCR["SCs"][:], 1.0, None, Alu.is_ge)
        vtt(Alu.add, SCR["SBs"][:], SCR["SBs"][:], SCR["SDs"][:])
        vts(SCR["SDs"][:], SCR["SCs"][:], 0.0, None, Alu.is_lt)
        vtt(Alu.subtract, SCR["SBs"][:], SCR["SBs"][:], SCR["SDs"][:])
        vts(SCR["SBs"][:], SCR["SBs"][:], -2.0, None, Alu.add, inc=True)
        act(SCR["SCs"][:], SCR["SBs"][:], AF.Exp, 0.0, C(LN2_HI), waits=[("VE", kb.c("VE"))])
        act(SCR["SDs"][:], SCR["SBs"][:], AF.Exp, 0.0, C(LN2_LO2))
        vtt(Alu.mult, SCR["SCs"][:], SCR["SCs"][:], SCR["SDs"][:], waits=[("SE", kb.c("SE"))])
        vts(SCR["SDs"][:], SCR["SAs"][:], 0.0, None, Alu.is_gt)
        vtt(Alu.mult, SCR["SCs"][:], SCR["SCs"][:], SCR["SDs"][:])
        vtt(Alu.subtract, SCR["SCs"][:], SCR["SCs"][:], SCR["SDs"][:])
        vts(SCR["SCs"][:], SCR["SCs"][:], 1.0, None, Alu.add)
        vrecip(SCR["SDs"][:], SCR["SCs"][:])
        kb.emit("vector", lambda e, s=src_ap,
                d=SCR["Qs"][:].rearrange("p (a b) -> p a b", b=32),
                b2=SCR["SDs"][:].unsqueeze(2).broadcast_to([128, 64, 32]):
                e.tensor_tensor(d, s.rearrange("p (a b) -> p a b", b=32),
                                b2, Alu.mult))
        vts(SCR["Qs"][:], SCR["Qs"][:], 6.0, -6.0, Alu.min, Alu.max, inc=True)
        act(SCR["TAs"][:], SCR["Qs"][:], AF.Square, 0.0, 1.0, waits=[("VE", kb.c("VE"))])
        vts(SCR["TBs"][:], SCR["TAs"][:], 16.0, None, Alu.is_ge, waits=[("SE", kb.c("SE"))])
        vts(SCR["TAs"][:], SCR["TAs"][:], 4.0, None, Alu.is_ge)
        vts(SCR["TBs"][:], SCR["TBs"][:], 0.5, 1.0, Alu.mult, Alu.add)
        vtt(Alu.mult, SCR["TAs"][:], SCR["TAs"][:], SCR["TBs"][:])
        vts(SCR["TAs"][:], SCR["TAs"][:], -1.0, 2.0, Alu.mult, Alu.add)
        vtt(Alu.mult, SCR["TBs"][:], SCR["Qs"][:], SCR["TAs"][:])
        vcopy(SCR["QIs"][:], SCR["TBs"][:])
        vcopy(SCR["Qs"][:], SCR["QIs"][:])
        vtt(Alu.subtract, SCR["TBs"][:], SCR["TBs"][:], SCR["Qs"][:])
        vts(SCR["Xs"][:], SCR["TBs"][:], 0.5, None, Alu.is_gt)
        vts(SCR["TBs"][:], SCR["TBs"][:], -0.5, None, Alu.is_lt)
        vtt(Alu.add, SCR["Qs"][:], SCR["Qs"][:], SCR["Xs"][:])
        vtt(Alu.subtract, SCR["Qs"][:], SCR["Qs"][:], SCR["TBs"][:])
        vrecip(SCR["TAs"][:], SCR["TAs"][:])
        vtt(Alu.mult, SCR["Qs"][:], SCR["Qs"][:], SCR["TAs"][:])
        kb.emit("vector", lambda e, d=dstq_ap,
                q=SCR["Qs"][:].rearrange("p (a b) -> p a b", b=32),
                b2=SCR["SCs"][:].unsqueeze(2).broadcast_to([128, 64, 32]):
                e.tensor_tensor(d.rearrange("p (a b) -> p a b", b=32), q,
                                b2, Alu.mult), inc="VE")

    with ExitStack() as p3:
        wq0 = p3.enter_context(nc.sbuf_tensor("wq0", [128, 1024], F32))
        wq1 = p3.enter_context(nc.sbuf_tensor("wq1", [128, 1024], F32))
        qs1 = p3.enter_context(nc.sbuf_tensor("qs1", [128, 1024], F32))
        qs2 = p3.enter_context(nc.sbuf_tensor("qs2", [128, 1024], F32))
        qsb0 = p3.enter_context(nc.sbuf_tensor("qsb0", [128, 1024], BF16))
        qsb1 = p3.enter_context(nc.sbuf_tensor("qsb1", [128, 1024], BF16))
        acc4 = p3.enter_context(nc.sbuf_tensor("acc4", [128, 4], F32))
        l1qc = p3.enter_context(nc.sbuf_tensor("l1qc", [128, 2048], BF16))

        MATS = [(wgTs, wgq_in, 0), (wuTs, wuq_in, 1), (wdTs, wdq_in, 2)]

        def chunks_of(src):
            rows, cols = src.shape
            nch = rows * cols // (128 * 1024)
            res = []
            for j in range(nch):
                if cols >= 1024:
                    cpr = cols // 1024
                    rr, cc = j // cpr, (j % cpr) * 1024
                    res.append((lambda t, rr=rr, cc=cc:
                                t[rr * 128:(rr + 1) * 128, cc:cc + 1024], False))
                else:
                    rpc = 1024 // cols
                    r0 = j * 128 * rpc
                    res.append((lambda t, r0=r0, rpc=rpc, cols=cols:
                                t[r0:r0 + 128 * rpc, :].rearrange(
                                    "(a p) c -> p a c", p=128), True))
            return res

        kb.emit("vector", lambda e: e.memset(acc4[:], 0.0), inc="VE",
                waits=[("VE", VE_CONST)])
        last_red = {}
        for (src, dst, col) in MATS:
            for jj, (slf, flat) in enumerate(chunks_of(src)):
                buf = wq0 if jj % 2 == 0 else wq1
                bufguard = last_red.get(jj % 2, 0)
                if flat:
                    ld = kb.dma("sync", buf[:].rearrange(
                        "p (a c) -> p a c", c=src.shape[1]), slf(src),
                        waits=[("VE", bufguard)])
                else:
                    ld = kb.dma("sync", buf[:], slf(src), waits=[("VE", bufguard)])
                kb.emit("vector", lambda e, b=buf: e.tensor_reduce(
                    row1[:, 6:7], b[:], AX.X, Alu.add, apply_absolute_value=True),
                    waits=[("LD", ld)])
                kb.emit("vector", lambda e, c=col: e.tensor_add(
                    acc4[:, c:c + 1], acc4[:, c:c + 1], row1[:, 6:7]), inc="VE")
                last_red[jj % 2] = kb.c("VE")
        mm(PS[6][0:4, 0:1], acc4[:, 0:4], ones_c[:, 0:1], True, True,
           waits=[("VE", kb.c("VE"))], inc="PE")
        vcopy(evb3[0:4, 0:1], PS[6][0:4, 0:1], waits=[("PE", kb.c("PE"))], inc=True)
        kb.dma("gpsimd", ar_in[0:4, 0:1], evb3[0:4, 0:1],
               waits=[("VE", kb.c("VE"))])
        kb.emit("gpsimd", lambda e: e.collective_compute(
            "AllReduce", Alu.add, replica_groups=[list(range(n_cores))],
            ins=[ar_in[:, :].opt()], outs=[ar_out[:, :].opt()]),
            waits=[("GD", kb.c("GD"))], inc="CC")
        kb.dma("gpsimd", sc_t[:, 0:4],
               ar_out[:, :].flatten().unsqueeze(0).broadcast_to([128, 4]),
               waits=[("CC", kb.c("CC"))])
        vts(sc_t[:, 0:3], sc_t[:, 0:3], INV2_24, None, Alu.mult,
            waits=[("GD", kb.c("GD"))])
        vts(sc_t[:, 0:3], sc_t[:, 0:3], 1e-5, None, Alu.max)
        vrecip(sc_t[:, 4:7], sc_t[:, 0:3])
        vts(sc_t[:, 7:8], sc_t[:, 1:2], FWHT_NORM, None, Alu.mult, inc=True)
        VE_SC = kb.c("VE")

        last_q = {}
        for (src, dst, col) in MATS:
            for jj, (slf, flat) in enumerate(chunks_of(src)):
                buf = wq0 if jj % 2 == 0 else wq1
                qb = qsb0 if jj % 2 == 0 else qsb1
                if flat:
                    ld = kb.dma("sync", buf[:].rearrange(
                        "p (a c) -> p a c", c=src.shape[1]), slf(src),
                        waits=[("VE", last_q.get(jj % 2, VE_SC))])
                else:
                    ld = kb.dma("sync", buf[:], slf(src),
                                waits=[("VE", last_q.get(jj % 2, VE_SC))])
                kb.emit("vector", lambda e, b=buf, c=col: e.tensor_single_scalar(
                    qs1[:], b[:], sc_t[:, 4 + c:5 + c], Alu.mult),
                    waits=[("LD", ld), ("VE", VE_SC), ("GD", kb.c("GD"))])
                vts(qs2[:], qs1[:], 0.5, None, Alu.is_gt)
                vts(qs1[:], qs1[:], -0.5, None, Alu.is_lt)
                vtt(Alu.subtract, qb[:], qs2[:], qs1[:], inc=True)
                last_q[jj % 2] = kb.c("VE")
                if flat:
                    kb.dma("gpsimd", slf(dst),
                           qb[:].rearrange("p (a c) -> p a c", c=dst.shape[1]),
                           waits=[("VE", kb.c("VE"))])
                else:
                    kb.dma("gpsimd", slf(dst), qb[:], waits=[("VE", kb.c("VE"))])
        for ch in range(OSL // 128):
            ld = kb.dma("sync", SCR["Xs"][:], l1w_s[ch * 128:(ch + 1) * 128, :],
                        waits=[("VE", kb.c("VE"))])
            mxfp4_chunk(SCR["Xs"][:], l1qc[:],
                        waits=[("LD", ld), ("PE", kb.c("PE"))])
            vev = kb.c("VE")
            for i in range(KD):
                pst = PS[6 + (i % 2)]
                tr(pst[:].bitcast(BF16)[:, 0:128], l1qc[:, i * 128:(i + 1) * 128],
                   bf=True, waits=[("VE", vev), ("GD", kb.c("GD")),
                                   ("SE", kb.c("SE"))])
                act(qsb0[:, 0:128].bitcast(BF16) if False else qsb0[:, 0:128],
                    pst[:].bitcast(BF16)[:, 0:128], AF.Copy, 0.0, 1.0,
                    waits=[("PE", kb.c("PE")), ("GD", kb.c("GD"))])
                kb.dma("gpsimd", l1q_in[i * 128:(i + 1) * 128, ch * 128:(ch + 1) * 128],
                       qsb0[:, 0:128], waits=[("SE", kb.c("SE"))])
        GD_WQ = kb.c("GD")
        for (iin, oo) in ((wgq_in, wgq_out), (wuq_in, wuq_out), (wdq_in, wdq_out),
                          (l1q_in, l1q_out)):
            kb.emit("gpsimd", lambda e, a=iin, b=oo: e.collective_compute(
                "AllGather", Alu.bypass, replica_groups=[list(range(n_cores))],
                ins=[a[:, :].opt()], outs=[b[:, :].opt()]),
                waits=[("GD", GD_WQ)], inc="CC")
    scrA.close()
    CC_W = kb.c("CC")
    kb.barrier()
    if upto <= 3:
        raise _Trunc(_ctx)

    kb.phase = "P45"
    p45 = ExitStack()
    KTV = p45.enter_context(nc.sbuf_tensor("KTV", [128, 2 * T], F32))
    QT = p45.enter_context(nc.sbuf_tensor("QT", [128, T], F32))
    BSL = [p45.enter_context(nc.sbuf_tensor(f"bsl{i}", [128, 1024], F32))
           for i in range(6)]
    atok = p45.enter_context(nc.sbuf_tensor("atok", [128, (T // 128) * NA], F32))
    dgtw = p45.enter_context(nc.sbuf_tensor("dgtw", [128, RQ * 128], F32))
    am_p = p45.enter_context(nc.sbuf_tensor("am_p", [NA, 1024], F32))
    Pt0 = p45.enter_context(nc.sbuf_tensor("Pt0", [128, QB], F32))
    Pt1 = p45.enter_context(nc.sbuf_tensor("Pt1", [128, QB], F32))
    Pt2 = p45.enter_context(nc.sbuf_tensor("Pt2", [128, QB], F32))
    sacc = p45.enter_context(nc.sbuf_tensor("sacc", [128, QB], F32))
    yob = p45.enter_context(nc.sbuf_tensor("yob", [128, QB], F32))

    for blk8 in range(T // 1024):
        ld = kb.dma("sync", am_p[:, :],
                    a_mine_dram[:, blk8 * 1024:(blk8 + 1) * 1024],
                    waits=[("GD", GD_AMINE), ("CC", CC_P2),
                           ("SE", kb.c("SE"))])
        for jj in range(8):
            tau = blk8 * 8 + jj
            tr(PS[6][:, 0:NA], am_p[0:NA, jj * 128:(jj + 1) * 128],
               waits=[("LD", ld), ("SE", kb.c("SE"))])
            act(atok[:, tau * NA:(tau + 1) * NA], PS[6][:, 0:NA], AF.Copy, 0.0, 1.0,
                waits=[("PE", kb.c("PE"))])
    SE_ATOK = kb.c("SE")

    def build_factor(dst_col, ranks, rank_base, hl, scale, token_major):
        se_base = kb.c("SE")
        n_done = [0]
        for blk in range(T // 1024):
            lds = []
            for i, r in enumerate(ranks):
                ld = kb.dma("scalar", BSL[i][:].rearrange("p (a c) -> p a c", c=128),
                            ag1_out[blk * 1024:(blk + 1) * 1024,
                                    r * 128:(r + 1) * 128].rearrange(
                                        "(a p) c -> p a c", p=128),
                            waits=[("SE", kb.c("SE")), ("CC", CC_P2)])
                lds.append(ld)
            for tt_ in range(8):
                tau = blk * 8 + tt_
                ps = PS[tau % 2]
                for i, r in enumerate(ranks):
                    acol = atok[:, tau * NA + rank_base + i * HPC + hl:
                                tau * NA + rank_base + i * HPC + hl + 1]
                    kb.emit("vector", lambda e, a=acol, s=scale, i=i:
                            e.tensor_scalar(dgtw[:, i * 128:(i + 1) * 128],
                                            identf_t[:], a, s,
                                            Alu.mult, Alu.mult),
                            waits=([("SE", SE_ATOK), ("PE", kb.c("PE"))]
                                   if i == 0 else ()), inc="VE")
                vev = kb.c("VE")
                for i, r in enumerate(ranks):
                    w = [("SD", lds[i])]
                    if i == 0:
                        w += [("VE", vev),
                              ("SE", se_base + max(0, n_done[0] - 1))]
                    dg = dgtw[:, i * 128:(i + 1) * 128]
                    if token_major:
                        mm(ps[:, 0:128], dg, BSL[i][:, tt_ * 128:(tt_ + 1) * 128],
                           start=(i == 0), stop=(i == len(ranks) - 1), waits=w,
                           inc="PE")
                    else:
                        mm(ps[:, 0:128], BSL[i][:, tt_ * 128:(tt_ + 1) * 128], dg,
                           start=(i == 0), stop=(i == len(ranks) - 1), waits=w,
                           inc="PE")
                act(dst_col(tau), ps[:, 0:128], AF.Copy, 0.0, 1.0,
                    waits=[("PE", kb.c("PE"))])
                n_done[0] += 1

    for hl in range(HPC):
        kb.phase = "P4build"
        build_factor(lambda tau: QT[:, tau * 128:(tau + 1) * 128],
                     list(range(RQ)), 0, hl, C(1.0 / RQ), False)
        build_factor(lambda tau: KTV[:, tau * 128:(tau + 1) * 128],
                     [RQ, RQ + 1], RQ * HPC, hl, 0.5, False)
        build_factor(lambda tau: KTV[:, T + tau * 128:T + (tau + 1) * 128],
                     [RQ + 2, RQ + 3], (RQ + 2) * HPC, hl, 0.5, True)
        SE_BUILD = kb.c("SE")
        if DEBUG_TAPS and hl == 0:
            kb.dma("sync", tap_qt[:, :], QT[:, :], waits=[("SE", SE_BUILD)])
            kb.dma("sync", tap_kt[:, :], KTV[:, 0:T], waits=[("SE", SE_BUILD)])
            kb.dma("sync", tap_v[:, :], KTV[:, T:2 * T], waits=[("SE", SE_BUILD)])

        kb.phase = "P5flash"
        SB3 = (PS[0], PS[1], PS[5])
        PT3 = (Pt0, Pt1, Pt2)
        for qb in range(NQB):
            kmax = 4 * (qb + 1)
            se0 = kb.c("SE")
            pe0 = kb.c("PE")
            ve0 = kb.c("VE")
            ld0 = kb.c("LD")
            mask_ve = {}

            def pe_after_qk(i):
                return pe0 + (i + 1) + max(0, i - 2)

            def emit_qk(i):
                ps = SB3[i % 3]
                w = [("SE", (se0 + i - 2) if i >= 3 else se0)]
                mm(ps[:, 0:QB], KTV[:, i * 128:(i + 1) * 128],
                   QT[:, qb * QB:(qb + 1) * QB], True, True, waits=w, inc="PE")

            addve = {}

            def emit_exp_mask(i):
                ps = SB3[i % 3]
                pt = PT3[i % 3]
                w = [("PE", pe_after_qk(i))]
                if i - 3 in addve:
                    w.append(("VE", addve[i - 3]))
                act(pt[:], ps[:, 0:QB], AF.Exp, C(-MSTAB), C(INVSQ), waits=w)
                if i >= 4 * qb:
                    jj = i - 4 * qb
                    vtt(Alu.mult, pt[:], pt[:], masks_t[:, jj * QB:(jj + 1) * QB],
                        waits=[("SE", kb.c("SE"))], inc=True)
                    mask_ve[i] = kb.c("VE")
                    wz = []
                else:
                    wz = [("SE", kb.c("SE"))]
                if i == 0:
                    vcopy(sacc[:], pt[:], waits=wz + [("PE", pe0)], inc=True)
                else:
                    vtt(Alu.add, sacc[:], sacc[:], pt[:], waits=wz, inc=True)
                addve[i] = kb.c("VE")

            def emit_av_z(i):
                pt = PT3[i % 3]
                if i >= 4 * qb:
                    w = [("VE", mask_ve[i])]
                else:
                    w = [("SE", se0 + i + 1)]
                if i == 0:
                    w = w + [("VE", ve0), ("SE", se0)]
                mm(PS[2][:, 0:QB], KTV[:, T + i * 128:T + (i + 1) * 128], pt[:],
                   start=(i == 0), stop=(i == kmax - 1), waits=w, inc="PE")

            for i in range(kmax):
                emit_qk(i)
                emit_exp_mask(i)
                if i >= 2:
                    emit_av_z(i - 2)
            emit_av_z(kmax - 2)
            emit_av_z(kmax - 1)
            PE_Z = kb.c("PE")
            mm(PS[3][0:1, 0:QB], ones_c[:, 0:1], sacc[:],
               True, True, waits=[("VE", addve[kmax - 1])], inc="PE")
            vrecip(rowq[0:1, 0:QB], PS[3][0:1, 0:QB],
                   waits=[("PE", kb.c("PE"))], inc=True)
            mm(PS[4][:, 0:QB], ones_r[0:1, 0:128], rowq[0:1, 0:QB], True, True,
               waits=[("VE", kb.c("VE")), ("SE", se0)], inc="PE")
            act(evb3[:, 0:QB], PS[4][:, 0:QB], AF.Copy, 0.0, 1.0,
                waits=[("PE", kb.c("PE"))])
            vtt(Alu.mult, yob[:], PS[2][:, 0:QB], evb3[:, 0:QB],
                waits=[("SE", kb.c("SE")), ("LD", ld0)], inc=True)
            kb.dma("sync", ag2_in[hl * 128:(hl + 1) * 128, qb * QB:(qb + 1) * QB],
                   yob[:], waits=[("VE", kb.c("VE"))])
    LD_Y = kb.c("LD")
    p45.close()

    kb.phase = "P5ag2"
    kb.emit("gpsimd", lambda e: e.collective_compute(
        "AllGather", Alu.bypass, replica_groups=[list(range(n_cores))],
        ins=[ag2_in[:, :].opt()], outs=[ag2_out[:, :].opt()]),
        waits=[("LD", LD_Y), ("CC", kb.c("CC"))], inc="CC")
    CC_AG2 = kb.c("CC")

    def sel_y(e):
        with e.register("rky") as rky:
            e.reg_load(rky, srk[0:1, 0:1])
            e.reg_mul(rky, rky, TC)
            srcap = bass.AP(ag2_out_h, rky,
                            [[HPC * DH * T, n_cores], [T, HPC * DH], [1, TC]])
            return e.dma_start(
                out=y_sel_dram[:, :].rearrange("(j r) c -> j r c", j=n_cores),
                in_=srcap)
    kb.emit("gpsimd", sel_y, waits=[("CC", CC_AG2), ("GD", kb.c("GD"))],
            inc="GD", inc_by=16)
    GD_YSEL = kb.c("GD")
    CC_Y = kb.c("CC")
    kb.barrier()
    if upto <= 5:
        raise _Trunc(_ctx)

    kb.phase = "P6"
    with ExitStack() as p6:
        yfull = p6.enter_context(nc.sbuf_tensor("yfull", [128, KD * TC], F32))
        wo0 = p6.enter_context(nc.sbuf_tensor("wo0", [128, KD * 128], F32))
        wo1 = p6.enter_context(nc.sbuf_tensor("wo1", [128, KD * 128], F32))
        sst = p6.enter_context(nc.sbuf_tensor("sst", [128, TC], F32))

        for k in range(KD):
            ldy = kb.dma("sync", yfull[:, k * TC:(k + 1) * TC],
                         y_sel_dram[k * 128:(k + 1) * 128, :],
                         waits=[("GD", GD_YSEL)] if k == 0 else ())
        wo_guard = {}
        for o in range(NO):
            wbuf = wo0 if o % 2 == 0 else wo1
            ldw = kb.dma("gpsimd", wbuf[:].rearrange("p (a c) -> p a c", c=128),
                         woT[:, o * 128:(o + 1) * 128].rearrange(
                             "(a p) c -> p a c", p=128),
                         waits=[("PE", wo_guard.get(o % 2, 0))])
            for th in range(TC // 512):
                ps = PS[0 + (th % 2)]
                se_guard = kb.c("SE")
                for k in range(KD):
                    mm(ps[:, 0:512],
                       wbuf[:, k * 128:(k + 1) * 128],
                       yfull[:, k * TC + th * 512:k * TC + (th + 1) * 512],
                       start=(k == 0), stop=(k == KD - 1),
                       waits=([("LD", ldy), ("GD", ldw),
                               ("SE", max(0, se_guard - SEOFF))]
                              if k == 0 else []),
                       inc="PE")
                act(sst[:, th * 512:(th + 1) * 512], ps[:, 0:512], AF.Copy, 0.0, 1.0,
                    waits=[("PE", kb.c("PE")), ("LD", kb.c("LD"))])
            kb.dma("sync", s_dram[o * 128:(o + 1) * 128, :], sst[:, 0:TC],
                   waits=[("SE", kb.c("SE"))])
            wo_guard[o % 2] = kb.c("PE")
    kb.barrier()
    if upto <= 6:
        raise _Trunc(_ctx)

    scrB = ExitStack()
    _ctx["stacks"].append(scrB)
    alloc_scr(scrB, "_b")
    ftb = scrB.enter_context(nc.sbuf_tensor("ftb", [128, TC], F32))
    fzb = scrB.enter_context(nc.sbuf_tensor("fzb", [128, TC], F32))

    kb.phase = "P7"
    with ExitStack() as p7:
        xqT = p7.enter_context(nc.sbuf_tensor("xqT", [128, KD * TC], BF16))
        wg0 = p7.enter_context(nc.sbuf_tensor("wg0", [128, KD * 128], BF16))
        wg1 = p7.enter_context(nc.sbuf_tensor("wg1", [128, KD * 128], BF16))
        wu0 = p7.enter_context(nc.sbuf_tensor("wu0", [128, KD * 128], BF16))
        wu1 = p7.enter_context(nc.sbuf_tensor("wu1", [128, KD * 128], BF16))
        ygb = p7.enter_context(nc.sbuf_tensor("ygb", [128, 512], F32))
        ugb0 = p7.enter_context(nc.sbuf_tensor("ugb0", [128, 512], F32))
        ugb1 = p7.enter_context(nc.sbuf_tensor("ugb1", [128, 512], F32))

        kb.dma("gpsimd", ftb[:], ft_dram.unsqueeze(0).broadcast_to([128, TC]))
        GD_FTB = kb.c("GD")
        for k in range(KD):
            kb.dma("sync", xqT[:, k * TC:(k + 1) * TC],
                   xnqT_dram[k * 128:(k + 1) * 128, 0:TC])
        LD_XQT = kb.c("LD")

        wg_guard = {}
        p7_ve = [0, 0]
        for g in range(GN):
            wgb = wg0 if g % 2 == 0 else wg1
            wub = wu0 if g % 2 == 0 else wu1
            j, lc = g // (GN // n_cores), (g % (GN // n_cores)) * 128
            ldg = kb.dma("gpsimd", wgb[:].rearrange("p (a c) -> p a c", c=128),
                         wgq_out[j, :, lc:lc + 128].rearrange("(a p) c -> p a c", p=128),
                         waits=[("CC", CC_W), ("PE", wg_guard.get(g % 2, 0))])
            ldu = kb.dma("gpsimd", wub[:].rearrange("p (a c) -> p a c", c=128),
                         wuq_out[j, :, lc:lc + 128].rearrange("(a p) c -> p a c", p=128))
            for H in range(TC // 512):
                psg = PS[0 + (H % 2)]
                psu = PS[2 + (H % 2)]
                psf = PS[4 + (H % 2)]
                for k in range(KD):
                    mm(psg[:, 0:512], wgb[:, k * 128:(k + 1) * 128],
                       xqT[:, k * TC + H * 512:k * TC + (H + 1) * 512],
                       start=(k == 0), stop=(k == KD - 1),
                       waits=([("GD", ldg), ("LD", LD_XQT), ("VE", p7_ve[-2])]
                              if k == 0 else []),
                       inc="PE" if k == KD - 1 else None)
                for k in range(KD):
                    mm(psu[:, 0:512], wub[:, k * 128:(k + 1) * 128],
                       xqT[:, k * TC + H * 512:k * TC + (H + 1) * 512],
                       start=(k == 0), stop=(k == KD - 1),
                       waits=[("GD", ldu)] if k == 0 else [],
                       inc="PE" if k == KD - 1 else None)
                pev = kb.c("PE")
                vtt(Alu.mult, ygb[:], psg[:, 0:512], ftb[:, H * 512:(H + 1) * 512],
                    waits=[("PE", pev), ("GD", GD_FTB)], inc=True)
                act(evb0[:, 0:512], ygb[:], AF.Silu, zero_c[:, 0:1], sc_t[:, 0:1],
                    waits=[("VE", kb.c("VE"))])
                vtt(Alu.mult, ygb[:], psu[:, 0:512], ftb[:, H * 512:(H + 1) * 512],
                    waits=[("SE", kb.c("SE"))])
                vtt(Alu.mult, ygb[:], ygb[:], evb0[:, 0:512], inc=True)
                p7_ve.append(kb.c("VE"))
                if DEBUG_TAPS and g == 0 and H == 0:
                    kb.dma("sync", tap_y[0:128, 0:512], ygb[:],
                           waits=[("VE", kb.c("VE"))])
                    kb.dma("sync", tap_g[0:128, 0:512], evb0[:, 0:512],
                           waits=[("VE", kb.c("VE"))])
                mm(psf[:, 0:512], h128_t[:], ygb[:], True, True,
                   waits=[("VE", kb.c("VE")), ("SE", kb.c("SE"))], inc="PE")
                ub = ugb0 if (g * 2 + H) % 2 == 0 else ugb1
                act(ub[:], psf[:, 0:512], AF.Copy, 0.0, 1.0,
                    waits=[("PE", kb.c("PE")), ("SD", max(0, kb.c("SD") - 16))])
                kb.dma("scalar", u_dram[g, :, H * 512:(H + 1) * 512], ub[:],
                       waits=[("SE", kb.c("SE"))])
            wg_guard[g % 2] = kb.c("PE")
    kb.barrier()
    if upto <= 7:
        raise _Trunc(_ctx)

    kb.phase = "P8"
    with ExitStack() as p8:
        zT = p8.enter_context(nc.sbuf_tensor("zT", [128, IFFN], F32))
        zqT = p8.enter_context(nc.sbuf_tensor("zqT", [128, KI * 512], BF16))
        us0 = p8.enter_context(nc.sbuf_tensor("us0", [64, 8 * 128], F32))
        us1 = p8.enter_context(nc.sbuf_tensor("us1", [64, 8 * 128], F32))
        wd0 = p8.enter_context(nc.sbuf_tensor("wd0", [128, KI * 128], BF16))
        sgb = p8.enter_context(nc.sbuf_tensor("sgb", [128, 512], F32))
        zqb = p8.enter_context(nc.sbuf_tensor("zqb", [128, 2048], BF16))

        wd_guard = {}
        zqT_guard = 0
        for H in range(TC // 512):
            for tc4 in range(4):
                tch = H * 4 + tc4
                psz = PS[0 + (tc4 % 2)]
                for i1g in range(16):
                    se_guard = kb.c("SE")
                    ub = us0 if i1g % 2 == 0 else us1
                    ld = kb.dma("sync", ub[:].rearrange("p (a c) -> p a c", c=128),
                                u_dram[0:GN, i1g * 8:(i1g + 1) * 8,
                                       tch * 128:(tch + 1) * 128],
                                waits=[("PE", max(0, kb.c("PE") - 8))])
                    for i1l in range(8):
                        mm(psz[:, i1l * 64:(i1l + 1) * 64],
                           ub[:, i1l * 128:(i1l + 1) * 128], h64_t[:],
                           True, True,
                           waits=([("LD", ld), ("SE", se_guard)]
                                  if i1l == 0 else []),
                           inc="PE")
                    act(zT[:, i1g * 512:(i1g + 1) * 512], psz[:, 0:512],
                        AF.Identity, zero_c[:, 0:1], sc_t[:, 7:8],
                        waits=[("PE", kb.c("PE"))])
                SE_Z = kb.c("SE")
                kb.emit("vector", lambda e: e.reduce_max(
                    row1[:, 2:3], zT[:], AX.X, apply_absolute_value=True),
                    waits=[("SE", SE_Z)])
                vts(row1[:, 3:4], row1[:, 2:3], EPS_Q, None, Alu.max)
                vrecip(row1[:, 4:5], row1[:, 3:4])
                vts(row1[:, 5:6], row1[:, 4:5], 128.0, None, Alu.mult)
                vts(row2[:, tc4:tc4 + 1], row1[:, 3:4], 0.0078125, None, Alu.mult,
                    inc=True, waits=[("PE", kb.c("PE"))])
                for pc in range(IFFN // 2048):
                    sl = slice(pc * 2048, (pc + 1) * 2048)
                    act(SCR["TAs"][:], zT[:, sl], AF.Identity, zero_c[:, 0:1], row1[:, 5:6],
                        waits=[("VE", kb.c("VE"))])
                    emit_round(SCR["TAs"][:], SCR["QIs"][:], SCR["TBs"][:], SCR["Xs"][:],
                               waits=[("SE", kb.c("SE"))])
                    vts(zqb[:], SCR["TAs"][:], 127.0, None, Alu.min, inc=True,
                        waits=[("PE", kb.c("PE"))])
                    vev = kb.c("VE")
                    for i in range(16):
                        ci = pc * 16 + i
                        pst = PS[4 + (i % 2)]
                        tr(pst[:].bitcast(BF16)[:, 0:128],
                           zqb[:, i * 128:(i + 1) * 128], bf=True,
                           waits=[("VE", vev), ("SE", max(0, kb.c("SE") - 1)),
                                  ("PE", zqT_guard)])
                        act(zqT[:, ci * 512 + tc4 * 128:ci * 512 + (tc4 + 1) * 128],
                            pst[:].bitcast(BF16)[:, 0:128], AF.Copy, 0.0, 1.0,
                            waits=[("PE", kb.c("PE"))])
            SE_ZQT = kb.c("SE")
            tr(PS[6][0:4, 0:128], row2[:, 0:4], waits=[("VE", kb.c("VE"))])
            vcopy(evb1[0:4, 0:128], PS[6][0:4, 0:128], waits=[("PE", kb.c("PE"))],
                  inc=True)
            kb.dma("sync",
                   fz_dram.rearrange("(a b) -> a b", b=128)[H * 4:(H + 1) * 4, :],
                   evb1[0:4, 0:128], waits=[("VE", kb.c("VE"))])
            kb.dma("gpsimd", fzb[:, H * 512:(H + 1) * 512],
                   fz_dram[H * 512:(H + 1) * 512].unsqueeze(0).broadcast_to([128, 512]),
                   waits=[("LD", kb.c("LD"))])
            GD_FZB = kb.c("GD")

            p8_ve = [0, 0] if H == 0 else p8_ve
            for o in range(NO):
                wdb = wd0
                j, lo = (o * 128) // OSL, (o * 128) % OSL
                ldw = kb.dma("gpsimd", wdb[:].rearrange("p (a c) -> p a c", c=128),
                             wdq_out[j, :, lo:lo + 128].rearrange(
                                 "(a p) c -> p a c", p=128),
                             waits=[("PE", wd_guard.get(0, 0))])
                psd = PS[2 + (o % 2)]
                for k in range(KI):
                    mm(psd[:, 0:512], wdb[:, k * 128:(k + 1) * 128],
                       zqT[:, k * 512:(k + 1) * 512],
                       start=(k == 0), stop=(k == KI - 1),
                       waits=([("GD", ldw), ("SE", SE_ZQT),
                               ("VE", p8_ve[-2])] if k == 0 else []),
                       inc="PE" if k == KI - 1 else None)
                wd_guard[0] = kb.c("PE")
                pev = kb.c("PE")
                vtt(Alu.mult, sgb[:], psd[:, 0:512], fzb[:, H * 512:(H + 1) * 512],
                    waits=[("PE", pev), ("GD", GD_FZB), ("LD", kb.c("LD"))],
                    inc=True)
                act(evb0[:, 0:512], sgb[:], AF.Silu, zero_c[:, 0:1], sc_t[:, 2:3],
                    waits=[("VE", kb.c("VE"))])
                ld = kb.dma("sync", evb1[:, 0:512],
                            s_dram[o * 128:(o + 1) * 128, H * 512:(H + 1) * 512],
                            waits=[("SE", kb.c("SE")), ("VE", kb.c("VE"))])
                vtt(Alu.mult, sgb[:], evb0[:, 0:512], evb1[:, 0:512],
                    waits=[("LD", ld), ("SE", kb.c("SE"))], inc=True)
                p8_ve.append(kb.c("VE"))
                kb.dma("sync", s_dram[o * 128:(o + 1) * 128, H * 512:(H + 1) * 512],
                       sgb[:], waits=[("VE", kb.c("VE"))])
            zqT_guard = kb.c("PE")
    kb.barrier()
    if upto <= 8:
        raise _Trunc(_ctx)

    _res_id = [0]

    def emit_residual(sg_src, x_src, dst_tok, per_chunk=None):
        _res_id[0] += 1
        res_ve = [0, 0]
        with ExitStack() as rp:
            xacc = rp.enter_context(nc.sbuf_tensor(f"xacc_r{_res_id[0]}", [128, D], F32))
            sgs = rp.enter_context(nc.sbuf_tensor(f"sgs_r{_res_id[0]}", [128, NO * 128], F32))
            for tch in range(TCH):
                ldx = kb.dma("sync", xacc[:], x_src[tch * 128:(tch + 1) * 128, :],
                             waits=[("VE", kb.c("VE")), ("OD", kb.c("OD")),
                                    ("SE", kb.c("SE"))])
                lds = kb.dma("sync", sgs[:].rearrange("p (a c) -> p a c", c=128),
                             sg_src[:, tch * 128:(tch + 1) * 128].rearrange(
                                 "(a p) c -> p a c", p=128),
                             waits=[("PE", kb.c("PE"))])
                for o in range(NO):
                    pst = PS[4 + (o % 2)]
                    tr(pst[:, 0:128], sgs[:, o * 128:(o + 1) * 128],
                       waits=[("LD", lds), ("VE", res_ve[-2])])
                    kb.emit("vector", lambda e, o=o, pst=pst: e.tensor_add(
                        xacc[:, o * 128:(o + 1) * 128],
                        xacc[:, o * 128:(o + 1) * 128], pst[0:128, 0:128]),
                        waits=[("PE", kb.c("PE")), ("LD", ldx)], inc="VE")
                    res_ve.append(kb.c("VE"))
                if per_chunk is not None:
                    per_chunk(tch, xacc)
                kb.dma("sync", dst_tok[tch * 128:(tch + 1) * 128, :], xacc[:],
                       waits=[("VE", kb.c("VE")), ("SE", kb.c("SE"))],
                       sem="OD" if dst_tok is out else "LD")

    kb.phase = "P8b"
    with ExitStack() as p8b:
        xn1c = p8b.enter_context(nc.sbuf_tensor("xn1c", [128, D], F32))
        aqc = p8b.enter_context(nc.sbuf_tensor("aqc", [128, D], BF16))
        asm2 = p8b.enter_context(nc.sbuf_tensor("asm2", [128, 128], F32))
        asb2 = p8b.enter_context(nc.sbuf_tensor("asb2", [128, 128], BF16))

        def l1_prep(tch, xacc):
            for sg in range(4):
                kb.emit("vector", lambda e, x=xacc, sg=sg: e.bn_stats(
                    bst[:, sg * 6:(sg + 1) * 6], x[:, sg * 512:(sg + 1) * 512]),
                    waits=([("VE", kb.c("VE")), ("SE", kb.c("SE"))] if sg == 0 else []))
            kb.emit("vector", lambda e: e.bn_aggr(mv_t[:], bst[:]), inc="VE")
            act(row1[:, 0:1], mv_t[:, 1:2], AF.Sqrt, C(EPS_LN), 1.0,
                waits=[("VE", kb.c("VE"))])
            vrecip(row1[:, 1:2], row1[:, 0:1], waits=[("SE", kb.c("SE"))])
            kb.emit("vector", lambda e, x=xacc: e.scalar_tensor_tensor(
                xn1c[:].rearrange("p (a b) -> p a b", b=D),
                x[:].rearrange("p (a b) -> p a b", b=D),
                mv_t[:, 0:1],
                row1[:, 1:2].unsqueeze(2).broadcast_to([128, 1, D]),
                Alu.subtract, Alu.mult),
                waits=[("PE", kb.c("PE")), ("SE", kb.c("SE"))], inc="VE")
            mxfp4_chunk(xn1c[:], aqc[:],
                        waits=[("VE", kb.c("VE")), ("PE", kb.c("PE"))])
            for i in range(KD):
                pst = PS[6 + (i % 2)]
                tr(pst[:, 0:128], xn1c[:, i * 128:(i + 1) * 128],
                   waits=[("VE", kb.c("VE")), ("LD", kb.c("LD")),
                          ("SE", kb.c("SE"))])
                act(asm2[:], pst[:, 0:128], AF.Copy, 0.0, 1.0,
                    waits=[("PE", kb.c("PE")), ("LD", kb.c("LD"))])
                kb.dma("sync",
                       xn1T_dram[i * 128:(i + 1) * 128, tch * 128:(tch + 1) * 128],
                       asm2[:], waits=[("SE", kb.c("SE"))])
                tr(pst[:].bitcast(BF16)[:, 0:128], aqc[:, i * 128:(i + 1) * 128],
                   bf=True, waits=[("LD", kb.c("LD")), ("SE", kb.c("SE"))])
                act(asb2[:], pst[:].bitcast(BF16)[:, 0:128], AF.Copy, 0.0, 1.0,
                    waits=[("PE", kb.c("PE")), ("LD", kb.c("LD"))])
                kb.dma("sync",
                       aqT_dram[i * 128:(i + 1) * 128, tch * 128:(tch + 1) * 128],
                       asb2[:], waits=[("SE", kb.c("SE"))])

        emit_residual(s_dram, x_in, xnew_dram, per_chunk=l1_prep)
    kb.barrier()
    if upto <= 9:
        raise _Trunc(_ctx)

    kb.phase = "P9"
    with ExitStack() as p9:
        wv0 = p9.enter_context(nc.sbuf_tensor("wv0", [128, KD * 128], F32))
        l1w0 = p9.enter_context(nc.sbuf_tensor("l1w0", [128, KD * 128], BF16))
        l1w1 = p9.enter_context(nc.sbuf_tensor("l1w1", [128, KD * 128], BF16))
        mvf = p9.enter_context(nc.sbuf_tensor("mvf", [128, KD * 512], F32))
        aqf = p9.enter_context(nc.sbuf_tensor("aqf", [128, KD * TC], BF16))
        v1t = p9.enter_context(nc.sbuf_tensor("v1t", [128, TC], F32))
        g1b = p9.enter_context(nc.sbuf_tensor("g1b", [128, 512], F32))

        ldwv = kb.dma("gpsimd", wv0[:].rearrange("p (a c) -> p a c", c=128),
                      l1wvT[:, :].rearrange("(a p) c -> p a c", p=128))
        for k in range(KD):
            ldaq = kb.dma("sync", aqf[:, k * TC:(k + 1) * TC],
                          aqT_dram[k * 128:(k + 1) * 128, :])
        for th in range(TC // 512):
            ps = PS[0 + (th % 2)]
            se_guard = kb.c("SE")
            pe_g = kb.c("PE")
            for k in range(KD):
                ld = kb.dma("sync", mvf[:, k * 512:(k + 1) * 512],
                            xn1T_dram[k * 128:(k + 1) * 128, th * 512:(th + 1) * 512],
                            waits=[("PE", pe_g)] if k == 0 else ())
                mm(ps[:, 0:512], wv0[:, k * 128:(k + 1) * 128],
                   mvf[:, k * 512:(k + 1) * 512],
                   start=(k == 0), stop=(k == KD - 1),
                   waits=([("LD", ld), ("GD", ldwv), ("SE", max(0, se_guard - SEOFF))]
                          if k == 0 else [("LD", ld)]),
                   inc="PE")
            act(v1t[:, th * 512:(th + 1) * 512], ps[:, 0:512], AF.Identity,
                l1bv_t[:, 0:1], ones_c[:, 0:1], waits=[("PE", kb.c("PE"))])
        SE_V1 = kb.c("SE")
        if upto <= 10:
            raise _Trunc(_ctx)

        l1_guard = {}
        for o in range(NO):
            wb = l1w0 if o % 2 == 0 else l1w1
            j, lo = (o * 128) // OSL, (o * 128) % OSL
            ldw = kb.dma("gpsimd", wb[:].rearrange("p (a c) -> p a c", c=128),
                         l1q_out[j, :, lo:lo + 128].rearrange("(a p) c -> p a c", p=128),
                         waits=[("PE", l1_guard.get(o % 2, 0))])
            for th in range(TC // 512):
                ps = PS[2 + (th % 2)]
                se_guard = kb.c("SE")
                for k in range(KD):
                    mm(ps[:, 0:512], wb[:, k * 128:(k + 1) * 128],
                       aqf[:, k * TC + th * 512:k * TC + (th + 1) * 512],
                       start=(k == 0), stop=(k == KD - 1),
                       waits=([("LD", ldaq), ("GD", ldw),
                               ("SE", max(0, se_guard - SEOFF))]
                              if k == 0 else []),
                       inc="PE")
                act(g1b[:], ps[:, 0:512], AF.Silu, l1bias_t[:, o:o + 1],
                    ones_c[:, 0:1],
                    waits=[("PE", kb.c("PE")), ("SE", SE_V1),
                           ("VE", kb.c("VE")), ("LD", kb.c("LD"))])
                vtt(Alu.mult, g1b[:], g1b[:], v1t[:, th * 512:(th + 1) * 512],
                    waits=[("SE", kb.c("SE")), ("LD", kb.c("LD"))], inc=True)
                kb.dma("sync",
                       s_dram[o * 128:(o + 1) * 128, th * 512:(th + 1) * 512],
                       g1b[:], waits=[("VE", kb.c("VE"))])
            l1_guard[o % 2] = kb.c("PE")
    kb.barrier()
    if upto <= 11:
        raise _Trunc(_ctx)
    if upto <= 12:
        dbg_res = dint("dbg_res", [TC, D])
        emit_residual(s_dram, xnew_dram, dbg_res)
        raise _Trunc(_ctx)
    if upto <= 13:
        emit_residual(s_dram, xnew_dram, out)
        raise _Trunc(_ctx)

    kb.phase = "RES"
    emit_residual(s_dram, xnew_dram, out)
    if DEBUG_DUMP:
        kb.wait("sync", "OD", kb.c("OD"))
        for nm, t in (("d_xnT", xnT_dram), ("d_ft", ft_dram),
                      ("d_amine", a_mine_dram), ("d_ag1", ag1_out),
                      ("d_ysel", y_sel_dram), ("d_u", u_dram),
                      ("d_fz", fz_dram), ("d_s", s_dram),
                      ("d_xnew", xnew_dram), ("d_xn1T", xn1T_dram),
                      ("d_xnqT", xnqT_dram)):
            sh = list(t.tensor.shape) if hasattr(t, "tensor") else None
            dt_ = t.tensor.dtype
            dbg = nc.dram_tensor(nm, sh, dt_, kind="ExternalOutput").ap()
            if len(sh) == 1:
                kb.dma("sync", dbg[:], t[:], sem="OD")
            elif len(sh) == 2:
                kb.dma("sync", dbg[:, :], t[:, :], sem="OD")
            else:
                kb.dma("sync", dbg[:, :, :], t[:, :, :], sem="OD")
    kb.wait("sync", "OD", kb.c("OD"))
    scrB.close()

    return _finish(nc, kb, top)


def _hadamard(n):
    h = np.array([[1.0]], dtype=np.float64)
    while h.shape[0] < n:
        h = np.block([[h, h], [h, -h]])
    return h


def _prep_inputs(inputs, n_cores, TC):
    T = n_cores * TC
    HPC = NH // n_cores
    NA = (RQ + RK + RV) * HPC
    CSL = IFFN // n_cores
    OSL = D // n_cores

    x = np.ascontiguousarray(np.asarray(inputs["L0_attn_x"], dtype=np.float32)
                             .reshape(T, D))
    W_q = np.asarray(inputs["L0_attn_W_q"], dtype=np.float32)
    W_kv = np.asarray(inputs["L0_attn_W_kv"], dtype=np.float32)
    W_o = np.asarray(inputs["L0_attn_W_o"], dtype=np.float32)
    w_gate = np.asarray(inputs["L0_ffn_w_gate"], dtype=np.float32)
    w_up = np.asarray(inputs["L0_ffn_w_up"], dtype=np.float32)
    w_down = np.asarray(inputs["L0_ffn_w_down"], dtype=np.float32)
    l1w = np.asarray(inputs["L1_ffn_weight"], dtype=np.float32)
    l1b = np.asarray(inputs["L1_ffn_bias"], dtype=np.float32)
    l1wkv = np.asarray(inputs["L1_attn_Wkv"], dtype=np.float32)
    l1bkv = np.asarray(inputs["L1_attn_bkv"], dtype=np.float32)

    acols = []
    for j in range(n_cores):
        for r in range(RQ):
            for hl in range(HPC):
                acols.append(("q", r * 144 + j * HPC + hl))
        for r in range(RK + RV):
            for hl in range(HPC):
                acols.append(("kv", r * 144 + j * HPC + hl))
    waT_np = np.empty((D, NA * n_cores), dtype=np.float32)
    for i, (kind, ri) in enumerate(acols):
        waT_np[:, i] = (W_q if kind == "q" else W_kv)[ri, :]
    bcols = []
    for r in range(RQ):
        bcols.extend(("q", r * 144 + 16 + d) for d in range(DH))
    for r in range(RK + RV):
        bcols.extend(("kv", r * 144 + 16 + d) for d in range(DH))
    wbT_np = np.empty((D, NB * DH), dtype=np.float32)
    for i, (kind, ri) in enumerate(bcols):
        wbT_np[:, i] = (W_q if kind == "q" else W_kv)[ri, :]

    woT_np = np.ascontiguousarray(W_o.T)
    perm = np.array([(cp % 128) * 64 + cp // 128 for cp in range(IFFN)])
    wgT_pm = np.ascontiguousarray(w_gate.T[:, perm])
    wuT_pm = np.ascontiguousarray(w_up.T[:, perm])
    wdT_np = np.ascontiguousarray(w_down.T)
    l1wvT_np = np.ascontiguousarray(l1wkv[DH:2 * DH, :].T)
    l1bv_np = np.ascontiguousarray(l1bkv[DH:2 * DH].reshape(DH, 1))

    identf_np = np.eye(128, dtype=np.float32)
    identb_np = np.eye(128, dtype=np.float32).astype(ml_dtypes.bfloat16)
    h128_np = _hadamard(128).astype(np.float32)
    h64_np = _hadamard(64).astype(np.float32)
    masks_np = np.zeros((128, 4 * 512), dtype=np.float32)
    for jj in range(4):
        for p in range(128):
            cstart = jj * 128 + p
            if cstart < 512:
                masks_np[p, jj * 512 + cstart:(jj + 1) * 512] = 1.0

    inv_freq = (1.0 / 10000.0) ** (np.arange(0, DH, 2, dtype=np.float64) / DH)

    in_maps = []
    for c in range(n_cores):
        t0 = c * TC
        tloc = np.arange(t0, t0 + TC, dtype=np.float64)
        fr = tloc[None, :] * inv_freq[:, None]
        cosd, sind = np.cos(fr), np.sin(fr)
        ch = cosd.astype(np.float32)
        cl = (cosd - ch.astype(np.float64)).astype(np.float32)
        sh = sind.astype(np.float32)
        sl = (sind - sh.astype(np.float64)).astype(np.float32)
        ctab_np = np.concatenate([np.tile(ch, (2, 1)), np.tile(cl, (2, 1))], axis=1)
        stab_np = np.concatenate([np.tile(sh, (2, 1)), np.tile(sl, (2, 1))], axis=1)
        im = {
            "x_slice": np.ascontiguousarray(x[t0:t0 + TC]),
            "waT": waT_np, "wbT": wbT_np, "woT": woT_np,
            "wgTs": np.ascontiguousarray(wgT_pm[:, c * CSL:(c + 1) * CSL]),
            "wuTs": np.ascontiguousarray(wuT_pm[:, c * CSL:(c + 1) * CSL]),
            "wdTs": np.ascontiguousarray(wdT_np[:, c * OSL:(c + 1) * OSL]),
            "l1w_s": np.ascontiguousarray(l1w[c * OSL:(c + 1) * OSL, :]),
            "l1wvT": l1wvT_np, "l1bv": l1bv_np,
            "l1bias": np.ascontiguousarray(l1b),
            "identf": identf_np, "identb": identb_np,
            "h128": h128_np, "h64": h64_np, "masks": masks_np,
            "ctab": np.ascontiguousarray(ctab_np),
            "stab": np.ascontiguousarray(stab_np),
            "rank_id": np.array([[c]], dtype=np.int32),
        }
        in_maps.append(im)
    return in_maps


def kernel(**inputs):
    n_cores = 8
    TC = 8192 // n_cores
    in_maps = _prep_inputs(inputs, n_cores, TC)
    nc = build_program(n_cores, TC)
    res = run_bass_kernel_spmd(nc, in_maps, core_ids=list(range(n_cores)))
    outs = [np.asarray(r["out"], dtype=np.float32) for r in res.results]
    return np.concatenate(outs, axis=0).reshape(1, 8192, D)


# version 08
# speedup vs baseline: 1.0235x; 1.0235x over previous
import numpy as np
import ml_dtypes
from contextlib import ExitStack

import concourse.bass as bass
import concourse.mybir as mybir
from concourse.bass_utils import run_bass_kernel_spmd

F32 = mybir.dt.float32
BF16 = mybir.dt.bfloat16
I32 = mybir.dt.int32
AF = mybir.ActivationFunctionType
Alu = mybir.AluOpType
AX = mybir.AxisListType

D = 2048
NH = 16
DH = 128
RQ, RK, RV = 6, 2, 2
NB = RQ + RK + RV
IFFN = 8192
GN = 64
MSTAB = 30.0
INVSQ = float(np.float32(1.0 / np.sqrt(128.0)))
FWHT_NORM = float(np.float32(1.0 / np.sqrt(8192.0)))
EPS_LN = 1e-5
EPS_Q = 1e-5
F32TINY = float(np.finfo(np.float32).tiny)
LN2 = 0.6931471805599453
LN2_HI = float(np.float32(LN2))
LN2_LO2 = float(np.float32(LN2 - np.float64(np.float32(LN2))))
IL2 = 1.4426950408889634
IL2_HI = float(np.float32(IL2))
IL2_LO = float(np.float32(IL2 - np.float64(np.float32(IL2))))
NUDGE = float(np.float32(2.0 ** -38))
INV2_24 = float(np.float32(2.0 ** -24))
DEBUG_TAPS = False
DEBUG_DUMP = False


class KB:
    ENGS = ("sync", "scalar", "vector", "tensor", "gpsimd")

    def __init__(self, nc):
        self.nc = nc
        self.ops = {e: [] for e in self.ENGS}
        self.semh = {}
        self.cnt = {}
        self.phase = "P0"

    def sem(self, name, stack):
        self.semh[name] = stack.enter_context(self.nc.semaphore(name))
        self.cnt[name] = 0

    def emit(self, eng, fn, waits=(), inc=None, inc_by=1, is_dma=False):
        semh = self.semh
        if eng in ("vector", "scalar") and not is_dma:
            sem = "VE" if eng == "vector" else "SE"
            waits = list(waits) + [(sem, self.cnt[sem])]
            inc, inc_by = sem, 1
        if inc is not None:
            self.cnt[inc] += inc_by
        waits = tuple((s, c) for (s, c) in waits if c > 0)

        def run(e, fn=fn, waits=waits, inc=inc, inc_by=inc_by):
            for s, c in waits:
                e.wait_ge(semh[s], c)
            ins = fn(e)
            if inc is not None:
                ins.then_inc(semh[inc], inc_by)
        self.ops[eng].append((self.phase, run))

    def dma(self, eng, dst, src, waits=(), sem=None, slow=False):
        sem = sem or {"gpsimd": "GD", "scalar": "SD"}.get(eng, "LD")
        waits = list(waits) + [(sem, self.cnt[sem])]
        self.emit(eng, lambda e, d=dst, s=src, sl=slow: e.dma_start(
            out=d, in_=s, allow_slow_non_contiguous=sl),
            waits=waits, inc=sem, inc_by=16, is_dma=True)
        return self.cnt[sem]

    def wait(self, eng, sem, val):
        semh = self.semh
        if val > 0:
            self.ops[eng].append(
                (self.phase, lambda e, s=sem, v=val: e.wait_ge(semh[s], v)))

    def c(self, s):
        return self.cnt[s]

    def barrier(self):
        snap = {s: self.cnt[s] for s in ("LD", "SD", "GD", "CC", "PE", "VE", "SE")}
        for eng in self.ENGS:
            for s, v in snap.items():
                self.wait(eng, s, v)



def _finish(nc, kb, top, close=True):
    def replay(e, lst):
        cur, scope = None, None
        for ph, f in lst:
            if ph != cur:
                if scope is not None:
                    nc.leave_named_scope(cur, scope, False)
                cur = ph
                scope, _ = nc.enter_named_scope(cur, False)
            f(e)
        if scope is not None:
            nc.leave_named_scope(cur, scope, False)

    with nc.Block() as blk:
        @blk.sync
        def _(e):
            replay(e, kb.ops["sync"])

        @blk.scalar
        def _(e):
            replay(e, kb.ops["scalar"])

        @blk.vector
        def _(e):
            replay(e, kb.ops["vector"])

        @blk.tensor
        def _(e):
            replay(e, kb.ops["tensor"])

        @blk.gpsimd
        def _(e):
            replay(e, kb.ops["gpsimd"])

    if close:
        top.close()
    return nc


class _Trunc(Exception):
    pass


def build_program(n_cores, TC, upto=99):
    try:
        return _build_program(n_cores, TC, upto)
    except _Trunc as t:
        c = t.args[0]
        kb, nc, top = c["kb"], c["nc"], c["top"]
        kb.dma("sync", c["out"][:, :], c["x_in"][:, :], sem="OD",
               waits=[("SE", kb.c("SE")), ("VE", kb.c("VE")),
                      ("PE", kb.c("PE")), ("GD", kb.c("GD")),
                      ("CC", kb.c("CC")), ("LD", kb.c("LD"))])
        kb.wait("sync", "OD", kb.c("OD"))
        if DEBUG_DUMP:
            for nm, tt in c["dumps"]:
                dbg = nc.dram_tensor(nm, list(tt.tensor.shape), tt.tensor.dtype,
                                     kind="ExternalOutput").ap()
                ix = tuple([slice(None)] * len(tt.tensor.shape))
                kb.dma("sync", dbg[ix], tt[ix], sem="OD")
            kb.wait("sync", "OD", kb.c("OD"))
        return _finish(nc, kb, top, close=False)


def _build_program(n_cores, TC, upto=99):
    T = n_cores * TC
    HPC = NH // n_cores
    NA = (RQ + RK + RV) * HPC
    NAall = NA * n_cores
    CSL = IFFN // n_cores
    OSL = D // n_cores
    TCH = TC // 128
    QB = 512
    NQB = T // QB
    KD = D // 128
    KI = IFFN // 128
    NO = D // 128

    SEOFF = 1 if (TC // 512) >= 2 else 0
    nc = bass.Bass("TRN2", target_bir_lowering=False,
                   detect_race_conditions=False)
    kb = KB(nc)
    top = ExitStack()

    def din(name, shape, dt=F32):
        return nc.dram_tensor(name, list(shape), dt, kind="ExternalInput").ap()

    def dint(name, shape, dt=F32):
        return nc.dram_tensor(name, list(shape), dt, kind="Internal").ap()

    x_in = din("x_slice", [TC, D])
    waT = din("waT", [D, NAall])
    wbT = din("wbT", [D, NB * DH])
    woT = din("woT", [D, D])
    wgTs = din("wgTs", [D, CSL])
    wuTs = din("wuTs", [D, CSL])
    wdTs = din("wdTs", [IFFN, OSL])
    l1w_s = din("l1w_s", [OSL, D])
    l1wvT = din("l1wvT", [D, DH])
    l1bv = din("l1bv", [DH, 1])
    l1bias = din("l1bias", [D])
    identf = din("identf", [128, 128])
    identb = din("identb", [128, 128], BF16)
    h128 = din("h128", [128, 128])
    h64 = din("h64", [64, 64])
    masks = din("masks", [128, 4 * QB])
    ctab_in = din("ctab", [128, 2 * TC])
    stab_in = din("stab", [128, 2 * TC])
    rank_in = din("rank_id", [1, 1], I32)
    out = nc.dram_tensor("out", [TC, D], F32, kind="ExternalOutput").ap()

    ag1_in = dint("ag1_in", [TC, NB * DH])
    ag1_out = dint("ag1_out", [T, NB * DH])
    ag1b_in = dint("ag1b_in", [NAall, TC])
    ag1b_out_h = nc.dram_tensor("ag1b_out", [n_cores * NAall, TC], F32,
                                kind="Internal")
    ag1b_out = ag1b_out_h.ap()
    ag2_in = dint("ag2_in", [HPC * DH, T])
    ag2_out_h = nc.dram_tensor("ag2_out", [n_cores * HPC * DH, T], F32,
                               kind="Internal")
    ag2_out = ag2_out_h.ap()
    a_mine_dram = dint("a_mine_dram", [NA, T])
    y_sel_dram = dint("y_sel_dram", [D, TC])
    ar_in = dint("ar_in", [4, 1])
    ar_out = dint("ar_out", [4, 1])
    wgq_in = dint("wgq_in", [D, CSL], BF16)
    wgq_out = dint("wgq_out", [n_cores, D, CSL], BF16)
    wuq_in = dint("wuq_in", [D, CSL], BF16)
    wuq_out = dint("wuq_out", [n_cores, D, CSL], BF16)
    wdq_in = dint("wdq_in", [IFFN, OSL], BF16)
    wdq_out = dint("wdq_out", [n_cores, IFFN, OSL], BF16)
    l1q_in = dint("l1q_in", [D, OSL], BF16)
    l1q_out = dint("l1q_out", [n_cores, D, OSL], BF16)
    xnT_dram = dint("xnT_dram", [D, TC])
    xnqT_dram = dint("xnqT_dram", [D, TC], BF16)
    u_dram = dint("u_dram", [GN, 128, TC])
    s_dram = dint("s_dram", [D, TC])
    ft_dram = dint("ft_dram", [TC])
    fz_dram = dint("fz_dram", [TC])
    xnew_dram = dint("xnew_dram", [TC, D])
    if DEBUG_TAPS:
        tap_qt = dint("tap_qt", [128, T])
        tap_kt = dint("tap_kt", [128, T])
        tap_v = dint("tap_v", [128, T])
        tap_g = dint("tap_g", [128, TC])
        tap_y = dint("tap_y", [128, TC])
    xn1T_dram = dint("xn1T_dram", [D, TC])
    aqT_dram = dint("aqT_dram", [D, TC], BF16)

    for s in ("LD", "SD", "GD", "CC", "PE", "VE", "SE", "OD"):
        kb.sem(s, top)

    sbuf = lambda n, sh, dt=F32: top.enter_context(nc.sbuf_tensor(n, list(sh), dt))
    cpool = sbuf("cpool", [128, 24])
    identf_t = sbuf("identf_t", [128, 128])
    identb_t = sbuf("identb_t", [128, 128], BF16)
    h128_t = sbuf("h128_t", [128, 128])
    h64_t = sbuf("h64_t", [64, 64])
    masks_t = sbuf("masks_t", [128, 4 * QB])
    ones_c = sbuf("ones_c", [128, 1])
    zero_c = sbuf("zero_c", [128, 1])
    ones_r = sbuf("ones_r", [1, 128])
    l1bv_t = sbuf("l1bv_t", [DH, 1])
    l1bias_t = sbuf("l1bias_t", [128, NO])
    sc_t = sbuf("sc_t", [128, 8])
    bst = sbuf("bst", [128, 24])
    mv_t = sbuf("mv_t", [128, 2])
    row1 = sbuf("row1", [128, 8])
    row2 = sbuf("row2", [128, 8])
    evb0 = sbuf("evb0", [128, 512])
    evb1 = sbuf("evb1", [128, 512])
    evb2 = sbuf("evb2", [128, 512])
    evb3 = sbuf("evb3", [128, 512])
    rowq = sbuf("rowq", [1, QB])
    srk = sbuf("srk", [1, 1], I32)
    SCR = {}

    def alloc_scr(stack, tag):
        for nm, sh, dt_ in (("Xs", [128, 2048], F32), ("Qs", [128, 2048], F32),
                            ("TAs", [128, 2048], F32), ("TBs", [128, 2048], F32),
                            ("QIs", [128, 2048], I32), ("SAs", [128, 64], F32),
                            ("SBs", [128, 64], F32), ("SCs", [128, 64], F32),
                            ("SDs", [128, 64], F32)):
            SCR[nm] = stack.enter_context(
                nc.sbuf_tensor(nm + tag, list(sh), dt_))

    PS = [top.enter_context(nc.psum_tensor(f"ps{i}", [128, 512], F32)) for i in range(8)]

    cvals = []
    def C(v):
        v32 = float(np.float32(v))
        if v32 not in cvals:
            cvals.append(v32)
        return v32
    for v in (-MSTAB, INVSQ, EPS_LN, LN2_HI, LN2_LO2, 1.0):
        C(v)
    assert len(cvals) <= 24
    for i, v in enumerate(cvals):
        nc.const_aps.register(F32, v, cpool[:, i:i + 1])
    for i, v in enumerate(cvals):
        kb.emit("vector", lambda e, v=v, i=i: e.memset(cpool[:, i:i + 1], v))
    kb.emit("vector", lambda e: e.memset(ones_c[:], 1.0))
    kb.emit("vector", lambda e: e.memset(zero_c[:], 0.0))
    kb.emit("vector", lambda e: e.memset(ones_r[:], 1.0), inc="VE")
    VE_CONST = kb.c("VE")

    def mm(ps_ap, lhs, rhs, start, stop, waits=(), inc=None):
        kb.emit("tensor", lambda e, o=ps_ap, l=lhs, r=rhs, a=start, b=stop:
                e.matmul(o, l, r, start=a, stop=b), waits=waits, inc=inc)

    def tr(ps_ap, in_ap, bf=False, waits=(), inc="PE"):
        idt = identb_t if bf else identf_t
        p = in_ap.partition_size()
        kb.emit("tensor", lambda e, o=ps_ap, i=in_ap, d=idt, p=p:
                e.transpose(o, i, d[0:p, 0:p]), waits=waits, inc=inc)

    def act(dst, src, func, bias, scale, waits=()):
        kb.emit("scalar", lambda e, d=dst, s=src, f=func, b=bias, sc=scale:
                e.activation(d, s, f, bias=b, scale=sc), waits=waits, inc="SE")

    def vcopy(dst, src, waits=(), inc=None):
        kb.emit("vector", lambda e, d=dst, s=src: e.tensor_copy(d, s),
                waits=waits, inc=("VE" if inc else None))

    def vtt(op, dst, a, b, waits=(), inc=None):
        kb.emit("vector", lambda e, d=dst, x=a, y=b, o=op: e.tensor_tensor(d, x, y, o),
                waits=waits, inc=("VE" if inc else None))

    def vts(dst, a, s1, s2, op1, op2=None, waits=(), inc=None):
        if s2 is None:
            kb.emit("vector", lambda e, d=dst, x=a, s=s1, o=op1:
                    e.tensor_single_scalar(d, x, s, o), waits=waits,
                    inc=("VE" if inc else None))
        else:
            kb.emit("vector", lambda e, d=dst, x=a, u=s1, v=s2, o1=op1, o2=op2:
                    e.tensor_scalar(d, x, u, v, o1, o2), waits=waits,
                    inc=("VE" if inc else None))

    def vstt(dst, in0, scalar, in1, op1, op2, waits=(), inc=None):
        kb.emit("vector", lambda e, d=dst, a=in0, s=scalar, b=in1, o1=op1, o2=op2:
                e.scalar_tensor_tensor(d, a, s, b, o1, o2), waits=waits,
                inc=("VE" if inc else None))

    def vrecip(dst, src, waits=(), inc=None):
        kb.emit("vector", lambda e, d=dst, s=src: e.reciprocal(d, s),
                waits=waits, inc=("VE" if inc else None))

    def emit_round(t_ap, i_ap, r_ap, g_ap, waits=()):
        vcopy(i_ap, t_ap, waits=waits)
        vcopy(r_ap, i_ap)
        vtt(Alu.subtract, g_ap, t_ap, r_ap)
        vts(t_ap, g_ap, 0.5, None, Alu.is_gt)
        vts(g_ap, g_ap, -0.5, None, Alu.is_lt)
        vtt(Alu.subtract, t_ap, t_ap, g_ap)
        vtt(Alu.add, t_ap, t_ap, r_ap)


    _ctx = {"nc": nc, "kb": kb, "top": top, "out": out, "x_in": x_in,
            "stacks": [],
            "dumps": [("d_xnT", xnT_dram), ("d_ft", ft_dram),
                      ("d_xnqT", xnqT_dram)]}

    kb.dma("gpsimd", srk[:], rank_in[:])
    GD_RANK = kb.c("GD")
    kb.dma("sync", identf_t[:], identf[:])
    kb.dma("sync", identb_t[:], identb[:])
    kb.dma("sync", h128_t[:], h128[:])
    kb.dma("sync", h64_t[:], h64[:])
    kb.dma("sync", masks_t[:], masks[:])
    kb.dma("sync", l1bv_t[:], l1bv[:])
    kb.dma("sync", l1bias_t[:], l1bias.rearrange("(a b) -> b a", b=128), slow=True)

    scrA = ExitStack()
    _ctx["stacks"].append(scrA)
    alloc_scr(scrA, "_a")
    with ExitStack() as pA:
        ctab_t = pA.enter_context(nc.sbuf_tensor("ctab_t", [128, TC], F32))
        stab_t = pA.enter_context(nc.sbuf_tensor("stab_t", [128, TC], F32))
        kb.dma("sync", ctab_t[:], ctab_in[:, 0:TC])
        kb.dma("sync", stab_t[:], stab_in[:, 0:TC])
        for src_t, dst_t in ((ctab_in, ctab_t), (stab_in, stab_t)):
            for i in range(TC // 512):
                ld = kb.dma("sync", evb0[:, 0:512],
                            src_t[:, TC + i * 512:TC + (i + 1) * 512],
                            waits=[("VE", kb.c("VE"))])
                kb.emit("vector", lambda e, i=i, d=dst_t: e.tensor_add(
                    d[:, i * 512:(i + 1) * 512], d[:, i * 512:(i + 1) * 512],
                    evb0[:, 0:512]), waits=[("LD", ld)], inc="VE")
        VE_TABS = kb.c("VE")

        kb.phase = "P1"
        with ExitStack() as p1:
            xtok = p1.enter_context(nc.sbuf_tensor("xtok", [128, TCH * D], F32))
            xnq = p1.enter_context(nc.sbuf_tensor("xnq", [128, TCH * D], BF16))
            asm = p1.enter_context(nc.sbuf_tensor("asm", [128, TC], F32))
            asmb = p1.enter_context(nc.sbuf_tensor("asmb", [128, TC], BF16))

            for i in range(TCH):
                kb.dma("sync", xtok[:, i * D:(i + 1) * D],
                       x_in[i * 128:(i + 1) * 128, :])
            LD_X = kb.c("LD")

            for i in range(TCH):
                xc = xtok[:, i * D:(i + 1) * D]
                for sg in range(4):
                    kb.emit("vector", lambda e, x=xc, sg=sg: e.bn_stats(
                        bst[:, sg * 6:(sg + 1) * 6], x[:, sg * 512:(sg + 1) * 512]),
                        waits=([("LD", LD_X), ("VE", VE_CONST),
                                ("SE", kb.c("SE"))] if sg == 0 else []))
                kb.emit("vector", lambda e: e.bn_aggr(mv_t[:], bst[:]), inc="VE")
                act(row1[:, 0:1], mv_t[:, 1:2], AF.Sqrt, C(EPS_LN), 1.0,
                    waits=[("VE", kb.c("VE"))])
                vrecip(row1[:, 1:2], row1[:, 0:1], waits=[("SE", kb.c("SE"))])
                kb.emit("vector", lambda e, x=xc: e.scalar_tensor_tensor(
                    x.rearrange("p (a b) -> p a b", b=D),
                    x.rearrange("p (a b) -> p a b", b=D),
                    mv_t[:, 0:1],
                    row1[:, 1:2].unsqueeze(2).broadcast_to([128, 1, D]),
                    Alu.subtract, Alu.mult), inc="VE")
                kb.emit("vector", lambda e, x=xc: e.reduce_max(
                    row1[:, 2:3], x, AX.X, apply_absolute_value=True))
                vts(row1[:, 3:4], row1[:, 2:3], EPS_Q, None, Alu.max)
                vrecip(row1[:, 4:5], row1[:, 3:4])
                vts(row1[:, 5:6], row1[:, 4:5], 128.0, None, Alu.mult)
                vts(row2[:, i:i + 1], row1[:, 3:4], 0.0078125, None, Alu.mult,
                    inc=True)
                act(SCR["TAs"][:], xc, AF.Identity, zero_c[:, 0:1], row1[:, 5:6],
                    waits=[("VE", kb.c("VE"))])
                emit_round(SCR["TAs"][:], SCR["QIs"][:], SCR["TBs"][:], SCR["Xs"][:], waits=[("SE", kb.c("SE"))])
                vts(xnq[:, i * D:(i + 1) * D], SCR["TAs"][:], 127.0, None, Alu.min, inc=True)
            VE_XQ = kb.c("VE")

            tr(PS[6][0:TCH, 0:128], row2[:, 0:TCH], waits=[("VE", VE_XQ)])
            vcopy(evb1[0:TCH, 0:128], PS[6][0:TCH, 0:128],
                  waits=[("PE", kb.c("PE"))], inc=True)
            kb.dma("sync", ft_dram.rearrange("(a b) -> a b", b=128),
                   evb1[0:TCH, 0:128], waits=[("VE", kb.c("VE"))])

            for i in range(KD):
                for j in range(TCH):
                    pst = PS[4 + (j % 2)]
                    tr(pst[:, 0:128], xtok[:, j * D + i * 128:j * D + (i + 1) * 128],
                       waits=[("VE", VE_XQ), ("SE", kb.c("SE"))])
                    act(asm[:, j * 128:(j + 1) * 128], pst[:, 0:128], AF.Copy,
                        0.0, 1.0, waits=[("PE", kb.c("PE")), ("LD", kb.c("LD"))])
                kb.dma("sync", xnT_dram[i * 128:(i + 1) * 128, 0:TC], asm[:, 0:TC],
                       waits=[("SE", kb.c("SE"))])
            for i in range(KD):
                for j in range(TCH):
                    pst = PS[4 + (j % 2)]
                    tr(pst[:].bitcast(BF16)[:, 0:128],
                       xnq[:, j * D + i * 128:j * D + (i + 1) * 128], bf=True,
                       waits=[("SE", kb.c("SE")), ("LD", kb.c("LD"))])
                    act(asmb[:, j * 128:(j + 1) * 128],
                        pst[:].bitcast(BF16)[:, 0:128], AF.Copy, 0.0, 1.0,
                        waits=[("PE", kb.c("PE")), ("LD", kb.c("LD"))])
                kb.dma("sync", xnqT_dram[i * 128:(i + 1) * 128, 0:TC], asmb[:, 0:TC],
                       waits=[("SE", kb.c("SE"))])
        kb.barrier()
        if upto <= 1:
            raise _Trunc(_ctx)

        kb.phase = "P2"
        with ExitStack() as p2:
            waT_t = p2.enter_context(nc.sbuf_tensor("waT_t", [128, KD * NAall], F32))
            aout = p2.enter_context(nc.sbuf_tensor("aout", [128, 2 * TC], F32))
            xnf = p2.enter_context(nc.sbuf_tensor("xnf", [128, KD * TC], F32))
            wb0 = p2.enter_context(nc.sbuf_tensor("wb0", [128, KD * 128], F32))
            wb1 = p2.enter_context(nc.sbuf_tensor("wb1", [128, KD * 128], F32))
            bscr = p2.enter_context(nc.sbuf_tensor("bscr", [128, TC], F32))
            byy = p2.enter_context(nc.sbuf_tensor("byy", [128, TC], F32))
            xs_t = p2.enter_context(nc.sbuf_tensor("xs_t", [128, TC], F32))

            kb.dma("gpsimd", waT_t[:].rearrange("p (a c) -> p a c", c=NAall),
                   waT[:, :].rearrange("(a p) c -> p a c", p=128))
            SD_WA = kb.c("GD")
            for k in range(KD):
                ldxn = kb.dma("sync", xnf[:, k * TC:(k + 1) * TC],
                              xnT_dram[k * 128:(k + 1) * 128, :])

            na_parts = [(0, min(128, NAall))]
            if NAall > 128:
                na_parts.append((128, NAall - 128))
            for (o0, osz) in na_parts:
                for th in range(TC // 512):
                    ps = PS[0 + (th % 2)]
                    se_guard = kb.c("SE")
                    for k in range(KD):
                        mm(ps[0:osz, 0:512],
                           waT_t[:, k * NAall + o0:k * NAall + o0 + osz],
                           xnf[:, k * TC + th * 512:k * TC + (th + 1) * 512],
                           start=(k == 0), stop=(k == KD - 1),
                           waits=([("LD", ldxn), ("GD", SD_WA),
                                   ("SE", max(0, se_guard - SEOFF))]
                                  if k == 0 else []),
                           inc="PE")
                    coff = (0 if o0 == 0 else TC) + th * 512
                    act(aout[0:osz, coff:coff + 512], ps[0:osz, 0:512],
                        AF.Copy, 0.0, 1.0, waits=[("PE", kb.c("PE"))])
            kb.dma("sync", ag1b_in[0:128, :], aout[:, 0:TC],
                   waits=[("SE", kb.c("SE"))])
            if NAall > 128:
                kb.dma("sync", ag1b_in[128:NAall, :], aout[0:NAall - 128, TC:2 * TC],
                       waits=[("SE", kb.c("SE"))])
            LD_A2A1 = kb.c("LD")

            wb_guard = {}
            for r in range(NB):
                wbuf = wb0 if r % 2 == 0 else wb1
                ldw = kb.dma("gpsimd", wbuf[:].rearrange("p (a c) -> p a c", c=128),
                             wbT[:, r * 128:(r + 1) * 128].rearrange(
                                 "(a p) c -> p a c", p=128),
                             waits=[("PE", wb_guard.get(r % 2, 0))])
                for th in range(TC // 512):
                    ps = PS[2 + (th % 2)]
                    se_guard = kb.c("SE")
                    for k in range(KD):
                        mm(ps[:, 0:512], wbuf[:, k * 128:(k + 1) * 128],
                           xnf[:, k * TC + th * 512:k * TC + (th + 1) * 512],
                           start=(k == 0), stop=(k == KD - 1),
                           waits=([("LD", ldxn), ("GD", ldw),
                                   ("SE", max(0, se_guard - SEOFF))]
                                  if k == 0 else []),
                           inc="PE")
                    act(bscr[:, th * 512:(th + 1) * 512], ps[:, 0:512],
                        AF.Copy, 0.0, 1.0, waits=[("PE", kb.c("PE"))])
                sev = kb.c("SE")
                if r >= RQ:
                    x1 = bscr[0:64, 0:TC]; x2 = bscr[64:128, 0:TC]
                    y1 = byy[0:64, 0:TC]; y2 = byy[64:128, 0:TC]
                    kb.dma("sync", xs_t[0:64, 0:TC], x2,
                           waits=[("SE", sev), ("VE", kb.c("VE"))])
                    ldsw = kb.dma("sync", xs_t[64:128, 0:TC], x1)
                    w0 = [("SE", sev), ("VE", VE_TABS)]
                    vtt(Alu.mult, y1, x1, ctab_t[0:64, 0:TC], waits=w0)
                    vtt(Alu.mult, y2, x2, ctab_t[64:128, 0:TC])
                    for th in range(TC // 512):
                        sl = slice(th * 512, (th + 1) * 512)
                        vtt(Alu.mult, evb0[0:64, 0:512], xs_t[0:64, sl],
                            stab_t[0:64, sl],
                            waits=[("LD", ldsw)] if th == 0 else ())
                        vtt(Alu.add, y1[:, sl], y1[:, sl], evb0[0:64, 0:512])
                        vtt(Alu.mult, evb0[64:128, 0:512], xs_t[64:128, sl],
                            stab_t[64:128, sl])
                        vtt(Alu.subtract, y2[:, sl], y2[:, sl], evb0[64:128, 0:512])
                    vcopy(x1, y1)
                    vcopy(x2, y2, inc=True)
                vev = kb.c("VE")
                for j in range(TCH):
                    pst = PS[4 + (j % 2)]
                    eb = evb1 if j % 2 == 0 else evb2
                    tr(pst[:, 0:128], bscr[:, j * 128:(j + 1) * 128],
                       waits=[("SE", sev), ("VE", vev), ("LD", kb.c("LD"))])
                    act(eb[:, 0:128], pst[:, 0:128], AF.Copy, 0.0, 1.0,
                        waits=[("PE", kb.c("PE"))])
                    kb.dma("sync",
                           ag1_in[j * 128:(j + 1) * 128, r * 128:(r + 1) * 128],
                           eb[:, 0:128], waits=[("SE", kb.c("SE"))])
                wb_guard[r % 2] = kb.c("PE")
            LD_AG1 = kb.c("LD")

            kb.emit("gpsimd", lambda e: e.collective_compute(
                "AllGather", Alu.bypass, replica_groups=[list(range(n_cores))],
                ins=[ag1_in[:, :].opt()], outs=[ag1_out[:, :].opt()]),
                waits=[("LD", LD_AG1)], inc="CC")
            kb.emit("gpsimd", lambda e: e.collective_compute(
                "AllGather", Alu.bypass, replica_groups=[list(range(n_cores))],
                ins=[ag1b_in[:, :].opt()], outs=[ag1b_out[:, :].opt()]),
                waits=[("LD", LD_A2A1)], inc="CC")
            CC_AG1B = kb.c("CC")
            def sel_a(e):
                with e.register("rka") as rka:
                    e.reg_load(rka, srk[0:1, 0:1])
                    e.reg_mul(rka, rka, NA * TC)
                    srcap = bass.AP(ag1b_out_h, rka,
                                    [[TC, NA], [NAall * TC, n_cores], [1, TC]])
                    return e.dma_start(
                        out=a_mine_dram[:, :].rearrange("r (j c) -> r j c", c=TC),
                        in_=srcap)
            kb.emit("gpsimd", sel_a,
                    waits=[("CC", CC_AG1B), ("GD", kb.c("GD"))],
                    inc="GD", inc_by=16)
        CC_P2 = kb.c("CC")
        GD_AMINE = kb.c("GD")
    kb.barrier()
    if upto <= 2:
        raise _Trunc(_ctx)

    kb.phase = "P3"
    def mxfp4_chunk(src_ap, dstq_ap, waits=()):
        kb.emit("vector", lambda e, s=src_ap, d=SCR["SAs"][:]: e.tensor_reduce(
            d, s.rearrange("p (a b) -> p a b", b=32), AX.X, Alu.max,
            apply_absolute_value=True), waits=waits)
        vts(SCR["SBs"][:], SCR["SAs"][:], F32TINY, None, Alu.max, inc=True)
        act(SCR["SBs"][:], SCR["SBs"][:], AF.Ln, 0.0, 1.0, waits=[("VE", kb.c("VE"))])
        vts(SCR["SCs"][:], SCR["SBs"][:], IL2_HI, NUDGE, Alu.mult, Alu.add,
            waits=[("SE", kb.c("SE"))])
        vstt(SCR["SCs"][:], SCR["SBs"][:], IL2_LO, SCR["SCs"][:], Alu.mult, Alu.add)
        vcopy(SCR["QIs"][:, 0:64], SCR["SCs"][:])
        vcopy(SCR["SBs"][:], SCR["QIs"][:, 0:64])
        vtt(Alu.subtract, SCR["SCs"][:], SCR["SCs"][:], SCR["SBs"][:])
        vts(SCR["SDs"][:], SCR["SCs"][:], 1.0, None, Alu.is_ge)
        vtt(Alu.add, SCR["SBs"][:], SCR["SBs"][:], SCR["SDs"][:])
        vts(SCR["SDs"][:], SCR["SCs"][:], 0.0, None, Alu.is_lt)
        vtt(Alu.subtract, SCR["SBs"][:], SCR["SBs"][:], SCR["SDs"][:])
        vts(SCR["SBs"][:], SCR["SBs"][:], -2.0, None, Alu.add, inc=True)
        act(SCR["SCs"][:], SCR["SBs"][:], AF.Exp, 0.0, C(LN2_HI), waits=[("VE", kb.c("VE"))])
        act(SCR["SDs"][:], SCR["SBs"][:], AF.Exp, 0.0, C(LN2_LO2))
        vtt(Alu.mult, SCR["SCs"][:], SCR["SCs"][:], SCR["SDs"][:], waits=[("SE", kb.c("SE"))])
        vts(SCR["SDs"][:], SCR["SAs"][:], 0.0, None, Alu.is_gt)
        vtt(Alu.mult, SCR["SCs"][:], SCR["SCs"][:], SCR["SDs"][:])
        vtt(Alu.subtract, SCR["SCs"][:], SCR["SCs"][:], SCR["SDs"][:])
        vts(SCR["SCs"][:], SCR["SCs"][:], 1.0, None, Alu.add)
        vrecip(SCR["SDs"][:], SCR["SCs"][:])
        kb.emit("vector", lambda e, s=src_ap,
                d=SCR["Qs"][:].rearrange("p (a b) -> p a b", b=32),
                b2=SCR["SDs"][:].unsqueeze(2).broadcast_to([128, 64, 32]):
                e.tensor_tensor(d, s.rearrange("p (a b) -> p a b", b=32),
                                b2, Alu.mult))
        vts(SCR["Qs"][:], SCR["Qs"][:], 6.0, -6.0, Alu.min, Alu.max, inc=True)
        act(SCR["TAs"][:], SCR["Qs"][:], AF.Square, 0.0, 1.0, waits=[("VE", kb.c("VE"))])
        vts(SCR["TBs"][:], SCR["TAs"][:], 16.0, None, Alu.is_ge, waits=[("SE", kb.c("SE"))])
        vts(SCR["TAs"][:], SCR["TAs"][:], 4.0, None, Alu.is_ge)
        vts(SCR["TBs"][:], SCR["TBs"][:], 0.5, 1.0, Alu.mult, Alu.add)
        vtt(Alu.mult, SCR["TAs"][:], SCR["TAs"][:], SCR["TBs"][:])
        vts(SCR["TAs"][:], SCR["TAs"][:], -1.0, 2.0, Alu.mult, Alu.add)
        vtt(Alu.mult, SCR["TBs"][:], SCR["Qs"][:], SCR["TAs"][:])
        vcopy(SCR["QIs"][:], SCR["TBs"][:])
        vcopy(SCR["Qs"][:], SCR["QIs"][:])
        vtt(Alu.subtract, SCR["TBs"][:], SCR["TBs"][:], SCR["Qs"][:])
        vts(SCR["Xs"][:], SCR["TBs"][:], 0.5, None, Alu.is_gt)
        vts(SCR["TBs"][:], SCR["TBs"][:], -0.5, None, Alu.is_lt)
        vtt(Alu.add, SCR["Qs"][:], SCR["Qs"][:], SCR["Xs"][:])
        vtt(Alu.subtract, SCR["Qs"][:], SCR["Qs"][:], SCR["TBs"][:])
        vrecip(SCR["TAs"][:], SCR["TAs"][:])
        vtt(Alu.mult, SCR["Qs"][:], SCR["Qs"][:], SCR["TAs"][:])
        kb.emit("vector", lambda e, d=dstq_ap,
                q=SCR["Qs"][:].rearrange("p (a b) -> p a b", b=32),
                b2=SCR["SCs"][:].unsqueeze(2).broadcast_to([128, 64, 32]):
                e.tensor_tensor(d.rearrange("p (a b) -> p a b", b=32), q,
                                b2, Alu.mult), inc="VE")

    with ExitStack() as p3:
        wq0 = p3.enter_context(nc.sbuf_tensor("wq0", [128, 1024], F32))
        wq1 = p3.enter_context(nc.sbuf_tensor("wq1", [128, 1024], F32))
        qs1 = p3.enter_context(nc.sbuf_tensor("qs1", [128, 1024], F32))
        qs2 = p3.enter_context(nc.sbuf_tensor("qs2", [128, 1024], F32))
        qsb0 = p3.enter_context(nc.sbuf_tensor("qsb0", [128, 1024], BF16))
        qsb1 = p3.enter_context(nc.sbuf_tensor("qsb1", [128, 1024], BF16))
        acc4 = p3.enter_context(nc.sbuf_tensor("acc4", [128, 4], F32))
        l1qc = p3.enter_context(nc.sbuf_tensor("l1qc", [128, 2048], BF16))

        MATS = [(wgTs, wgq_in, 0), (wuTs, wuq_in, 1), (wdTs, wdq_in, 2)]

        def chunks_of(src):
            rows, cols = src.shape
            nch = rows * cols // (128 * 1024)
            res = []
            for j in range(nch):
                if cols >= 1024:
                    cpr = cols // 1024
                    rr, cc = j // cpr, (j % cpr) * 1024
                    res.append((lambda t, rr=rr, cc=cc:
                                t[rr * 128:(rr + 1) * 128, cc:cc + 1024], False))
                else:
                    rpc = 1024 // cols
                    r0 = j * 128 * rpc
                    res.append((lambda t, r0=r0, rpc=rpc, cols=cols:
                                t[r0:r0 + 128 * rpc, :].rearrange(
                                    "(a p) c -> p a c", p=128), True))
            return res

        kb.emit("vector", lambda e: e.memset(acc4[:], 0.0), inc="VE",
                waits=[("VE", VE_CONST)])
        last_red = {}
        for (src, dst, col) in MATS:
            for jj, (slf, flat) in enumerate(chunks_of(src)):
                buf = wq0 if jj % 2 == 0 else wq1
                bufguard = last_red.get(jj % 2, 0)
                if flat:
                    ld = kb.dma("sync", buf[:].rearrange(
                        "p (a c) -> p a c", c=src.shape[1]), slf(src),
                        waits=[("VE", bufguard)])
                else:
                    ld = kb.dma("sync", buf[:], slf(src), waits=[("VE", bufguard)])
                kb.emit("vector", lambda e, b=buf: e.tensor_reduce(
                    row1[:, 6:7], b[:], AX.X, Alu.add, apply_absolute_value=True),
                    waits=[("LD", ld)])
                kb.emit("vector", lambda e, c=col: e.tensor_add(
                    acc4[:, c:c + 1], acc4[:, c:c + 1], row1[:, 6:7]), inc="VE")
                last_red[jj % 2] = kb.c("VE")
        mm(PS[6][0:4, 0:1], acc4[:, 0:4], ones_c[:, 0:1], True, True,
           waits=[("VE", kb.c("VE"))], inc="PE")
        vcopy(evb3[0:4, 0:1], PS[6][0:4, 0:1], waits=[("PE", kb.c("PE"))], inc=True)
        kb.dma("gpsimd", ar_in[0:4, 0:1], evb3[0:4, 0:1],
               waits=[("VE", kb.c("VE"))])
        kb.emit("gpsimd", lambda e: e.collective_compute(
            "AllReduce", Alu.add, replica_groups=[list(range(n_cores))],
            ins=[ar_in[:, :].opt()], outs=[ar_out[:, :].opt()]),
            waits=[("GD", kb.c("GD"))], inc="CC")
        kb.dma("gpsimd", sc_t[:, 0:4],
               ar_out[:, :].flatten().unsqueeze(0).broadcast_to([128, 4]),
               waits=[("CC", kb.c("CC"))])
        vts(sc_t[:, 0:3], sc_t[:, 0:3], INV2_24, None, Alu.mult,
            waits=[("GD", kb.c("GD"))])
        vts(sc_t[:, 0:3], sc_t[:, 0:3], 1e-5, None, Alu.max)
        vrecip(sc_t[:, 4:7], sc_t[:, 0:3])
        vts(sc_t[:, 7:8], sc_t[:, 1:2], FWHT_NORM, None, Alu.mult, inc=True)
        VE_SC = kb.c("VE")

        last_q = {}
        for (src, dst, col) in MATS:
            for jj, (slf, flat) in enumerate(chunks_of(src)):
                buf = wq0 if jj % 2 == 0 else wq1
                qb = qsb0 if jj % 2 == 0 else qsb1
                if flat:
                    ld = kb.dma("sync", buf[:].rearrange(
                        "p (a c) -> p a c", c=src.shape[1]), slf(src),
                        waits=[("VE", last_q.get(jj % 2, VE_SC))])
                else:
                    ld = kb.dma("sync", buf[:], slf(src),
                                waits=[("VE", last_q.get(jj % 2, VE_SC))])
                kb.emit("vector", lambda e, b=buf, c=col: e.tensor_single_scalar(
                    qs1[:], b[:], sc_t[:, 4 + c:5 + c], Alu.mult),
                    waits=[("LD", ld), ("VE", VE_SC), ("GD", kb.c("GD"))])
                vts(qs2[:], qs1[:], 0.5, None, Alu.is_gt)
                vts(qs1[:], qs1[:], -0.5, None, Alu.is_lt)
                vtt(Alu.subtract, qb[:], qs2[:], qs1[:], inc=True)
                last_q[jj % 2] = kb.c("VE")
                if flat:
                    kb.dma("gpsimd", slf(dst),
                           qb[:].rearrange("p (a c) -> p a c", c=dst.shape[1]),
                           waits=[("VE", kb.c("VE"))])
                else:
                    kb.dma("gpsimd", slf(dst), qb[:], waits=[("VE", kb.c("VE"))])
        for ch in range(OSL // 128):
            ld = kb.dma("sync", SCR["Xs"][:], l1w_s[ch * 128:(ch + 1) * 128, :],
                        waits=[("VE", kb.c("VE"))])
            mxfp4_chunk(SCR["Xs"][:], l1qc[:],
                        waits=[("LD", ld), ("PE", kb.c("PE"))])
            vev = kb.c("VE")
            for i in range(KD):
                pst = PS[6 + (i % 2)]
                tr(pst[:].bitcast(BF16)[:, 0:128], l1qc[:, i * 128:(i + 1) * 128],
                   bf=True, waits=[("VE", vev), ("GD", kb.c("GD")),
                                   ("SE", kb.c("SE"))])
                act(qsb0[:, 0:128].bitcast(BF16) if False else qsb0[:, 0:128],
                    pst[:].bitcast(BF16)[:, 0:128], AF.Copy, 0.0, 1.0,
                    waits=[("PE", kb.c("PE")), ("GD", kb.c("GD"))])
                kb.dma("gpsimd", l1q_in[i * 128:(i + 1) * 128, ch * 128:(ch + 1) * 128],
                       qsb0[:, 0:128], waits=[("SE", kb.c("SE"))])
        GD_WQ = kb.c("GD")
        for (iin, oo) in ((wgq_in, wgq_out), (wuq_in, wuq_out), (wdq_in, wdq_out),
                          (l1q_in, l1q_out)):
            kb.emit("gpsimd", lambda e, a=iin, b=oo: e.collective_compute(
                "AllGather", Alu.bypass, replica_groups=[list(range(n_cores))],
                ins=[a[:, :].opt()], outs=[b[:, :].opt()]),
                waits=[("GD", GD_WQ)], inc="CC")
    scrA.close()
    CC_W = kb.c("CC")
    kb.barrier()
    if upto <= 3:
        raise _Trunc(_ctx)

    kb.phase = "P45"
    p45 = ExitStack()
    KTV = p45.enter_context(nc.sbuf_tensor("KTV", [128, 2 * T], F32))
    QT = p45.enter_context(nc.sbuf_tensor("QT", [128, T], F32))
    bslw = p45.enter_context(nc.sbuf_tensor("bslw", [128, 8 * RQ * 128], F32))
    atok = p45.enter_context(nc.sbuf_tensor("atok", [128, (T // 128) * NA], F32))
    dgtw = p45.enter_context(nc.sbuf_tensor("dgtw", [128, RQ * 128], F32))
    am_p = p45.enter_context(nc.sbuf_tensor("am_p", [NA, 1024], F32))
    Pt0 = p45.enter_context(nc.sbuf_tensor("Pt0", [128, QB], F32))
    Pt1 = p45.enter_context(nc.sbuf_tensor("Pt1", [128, QB], F32))
    Pt2 = p45.enter_context(nc.sbuf_tensor("Pt2", [128, QB], F32))
    sacc = p45.enter_context(nc.sbuf_tensor("sacc", [128, QB], F32))
    yob = p45.enter_context(nc.sbuf_tensor("yob", [128, QB], F32))

    for blk8 in range(T // 1024):
        ld = kb.dma("sync", am_p[:, :],
                    a_mine_dram[:, blk8 * 1024:(blk8 + 1) * 1024],
                    waits=[("GD", GD_AMINE), ("CC", CC_P2),
                           ("SE", kb.c("SE"))])
        for jj in range(8):
            tau = blk8 * 8 + jj
            tr(PS[6][:, 0:NA], am_p[0:NA, jj * 128:(jj + 1) * 128],
               waits=[("LD", ld), ("SE", kb.c("SE"))])
            act(atok[:, tau * NA:(tau + 1) * NA], PS[6][:, 0:NA], AF.Copy, 0.0, 1.0,
                waits=[("PE", kb.c("PE"))])
    SE_ATOK = kb.c("SE")

    def build_factor(dst_col, ranks, rank_base, hl, scale, token_major):
        se_base = kb.c("SE")
        n_done = [0]
        nr = len(ranks)
        r0 = ranks[0]
        for blk in range(T // 1024):
            ld = kb.dma("scalar",
                        bslw[:, 0:nr * 8 * 128].rearrange(
                            "p (a c) -> p a c", c=nr * 128),
                        ag1_out[blk * 1024:(blk + 1) * 1024,
                                r0 * 128:(r0 + nr) * 128].rearrange(
                                    "(a p) c -> p a c", p=128),
                        waits=[("SE", kb.c("SE")), ("CC", CC_P2)])
            for tt_ in range(8):
                tau = blk * 8 + tt_
                ps = PS[tau % 2]
                for i, r in enumerate(ranks):
                    acol = atok[:, tau * NA + rank_base + i * HPC + hl:
                                tau * NA + rank_base + i * HPC + hl + 1]
                    kb.emit("vector", lambda e, a=acol, s=scale, i=i:
                            e.tensor_scalar(dgtw[:, i * 128:(i + 1) * 128],
                                            identf_t[:], a, s,
                                            Alu.mult, Alu.mult),
                            waits=([("SE", SE_ATOK), ("PE", kb.c("PE"))]
                                   if i == 0 else ()), inc="VE")
                vev = kb.c("VE")
                for i, r in enumerate(ranks):
                    w = []
                    if i == 0:
                        w = [("SD", ld), ("VE", vev),
                             ("SE", se_base + max(0, n_done[0] - 1))]
                    dg = dgtw[:, i * 128:(i + 1) * 128]
                    bs = bslw[:, tt_ * nr * 128 + i * 128:
                              tt_ * nr * 128 + (i + 1) * 128]
                    if token_major:
                        mm(ps[:, 0:128], dg, bs,
                           start=(i == 0), stop=(i == nr - 1), waits=w,
                           inc="PE")
                    else:
                        mm(ps[:, 0:128], bs, dg,
                           start=(i == 0), stop=(i == nr - 1), waits=w,
                           inc="PE")
                act(dst_col(tau), ps[:, 0:128], AF.Copy, 0.0, 1.0,
                    waits=[("PE", kb.c("PE"))])
                n_done[0] += 1

    for hl in range(HPC):
        kb.phase = "P4build"
        build_factor(lambda tau: QT[:, tau * 128:(tau + 1) * 128],
                     list(range(RQ)), 0, hl, C(1.0 / RQ), False)
        build_factor(lambda tau: KTV[:, tau * 128:(tau + 1) * 128],
                     [RQ, RQ + 1], RQ * HPC, hl, 0.5, False)
        build_factor(lambda tau: KTV[:, T + tau * 128:T + (tau + 1) * 128],
                     [RQ + 2, RQ + 3], (RQ + 2) * HPC, hl, 0.5, True)
        SE_BUILD = kb.c("SE")
        if DEBUG_TAPS and hl == 0:
            kb.dma("sync", tap_qt[:, :], QT[:, :], waits=[("SE", SE_BUILD)])
            kb.dma("sync", tap_kt[:, :], KTV[:, 0:T], waits=[("SE", SE_BUILD)])
            kb.dma("sync", tap_v[:, :], KTV[:, T:2 * T], waits=[("SE", SE_BUILD)])

        kb.phase = "P5flash"
        SB3 = (PS[0], PS[1], PS[5])
        PT3 = (Pt0, Pt1, Pt2)
        for qb in range(NQB):
            kmax = 4 * (qb + 1)
            se0 = kb.c("SE")
            pe0 = kb.c("PE")
            ve0 = kb.c("VE")
            ld0 = kb.c("LD")
            mask_ve = {}

            def pe_after_qk(i):
                return pe0 + (i + 1) + max(0, i - 2)

            def emit_qk(i):
                ps = SB3[i % 3]
                w = [("SE", (se0 + i - 2) if i >= 3 else se0)]
                mm(ps[:, 0:QB], KTV[:, i * 128:(i + 1) * 128],
                   QT[:, qb * QB:(qb + 1) * QB], True, True, waits=w, inc="PE")

            addve = {}

            def emit_exp_mask(i):
                ps = SB3[i % 3]
                pt = PT3[i % 3]
                w = [("PE", pe_after_qk(i))]
                if i - 3 in addve:
                    w.append(("VE", addve[i - 3]))
                act(pt[:], ps[:, 0:QB], AF.Exp, C(-MSTAB), C(INVSQ), waits=w)
                if i >= 4 * qb:
                    jj = i - 4 * qb
                    vtt(Alu.mult, pt[:], pt[:], masks_t[:, jj * QB:(jj + 1) * QB],
                        waits=[("SE", kb.c("SE"))], inc=True)
                    mask_ve[i] = kb.c("VE")
                    wz = []
                else:
                    wz = [("SE", kb.c("SE"))]
                if i == 0:
                    vcopy(sacc[:], pt[:], waits=wz + [("PE", pe0)], inc=True)
                else:
                    vtt(Alu.add, sacc[:], sacc[:], pt[:], waits=wz, inc=True)
                addve[i] = kb.c("VE")

            def emit_av_z(i):
                pt = PT3[i % 3]
                if i >= 4 * qb:
                    w = [("VE", mask_ve[i])]
                else:
                    w = [("SE", se0 + i + 1)]
                if i == 0:
                    w = w + [("VE", ve0), ("SE", se0)]
                mm(PS[2][:, 0:QB], KTV[:, T + i * 128:T + (i + 1) * 128], pt[:],
                   start=(i == 0), stop=(i == kmax - 1), waits=w, inc="PE")

            for i in range(kmax):
                emit_qk(i)
                emit_exp_mask(i)
                if i >= 2:
                    emit_av_z(i - 2)
            emit_av_z(kmax - 2)
            emit_av_z(kmax - 1)
            PE_Z = kb.c("PE")
            mm(PS[3][0:1, 0:QB], ones_c[:, 0:1], sacc[:],
               True, True, waits=[("VE", addve[kmax - 1])], inc="PE")
            vrecip(rowq[0:1, 0:QB], PS[3][0:1, 0:QB],
                   waits=[("PE", kb.c("PE"))], inc=True)
            mm(PS[4][:, 0:QB], ones_r[0:1, 0:128], rowq[0:1, 0:QB], True, True,
               waits=[("VE", kb.c("VE")), ("SE", se0)], inc="PE")
            act(evb3[:, 0:QB], PS[4][:, 0:QB], AF.Copy, 0.0, 1.0,
                waits=[("PE", kb.c("PE"))])
            vtt(Alu.mult, yob[:], PS[2][:, 0:QB], evb3[:, 0:QB],
                waits=[("SE", kb.c("SE")), ("LD", ld0)], inc=True)
            kb.dma("sync", ag2_in[hl * 128:(hl + 1) * 128, qb * QB:(qb + 1) * QB],
                   yob[:], waits=[("VE", kb.c("VE"))])
    LD_Y = kb.c("LD")
    p45.close()

    kb.phase = "P5ag2"
    kb.emit("gpsimd", lambda e: e.collective_compute(
        "AllGather", Alu.bypass, replica_groups=[list(range(n_cores))],
        ins=[ag2_in[:, :].opt()], outs=[ag2_out[:, :].opt()]),
        waits=[("LD", LD_Y), ("CC", kb.c("CC"))], inc="CC")
    CC_AG2 = kb.c("CC")

    def sel_y(e):
        with e.register("rky") as rky:
            e.reg_load(rky, srk[0:1, 0:1])
            e.reg_mul(rky, rky, TC)
            srcap = bass.AP(ag2_out_h, rky,
                            [[HPC * DH * T, n_cores], [T, HPC * DH], [1, TC]])
            return e.dma_start(
                out=y_sel_dram[:, :].rearrange("(j r) c -> j r c", j=n_cores),
                in_=srcap)
    kb.emit("gpsimd", sel_y, waits=[("CC", CC_AG2), ("GD", kb.c("GD"))],
            inc="GD", inc_by=16)
    GD_YSEL = kb.c("GD")
    CC_Y = kb.c("CC")
    kb.barrier()
    if upto <= 5:
        raise _Trunc(_ctx)

    kb.phase = "P6"
    with ExitStack() as p6:
        yfull = p6.enter_context(nc.sbuf_tensor("yfull", [128, KD * TC], F32))
        wo0 = p6.enter_context(nc.sbuf_tensor("wo0", [128, KD * 128], F32))
        wo1 = p6.enter_context(nc.sbuf_tensor("wo1", [128, KD * 128], F32))
        sst = p6.enter_context(nc.sbuf_tensor("sst", [128, TC], F32))

        for k in range(KD):
            ldy = kb.dma("sync", yfull[:, k * TC:(k + 1) * TC],
                         y_sel_dram[k * 128:(k + 1) * 128, :],
                         waits=[("GD", GD_YSEL)] if k == 0 else ())
        wo_guard = {}
        for o in range(NO):
            wbuf = wo0 if o % 2 == 0 else wo1
            ldw = kb.dma("gpsimd", wbuf[:].rearrange("p (a c) -> p a c", c=128),
                         woT[:, o * 128:(o + 1) * 128].rearrange(
                             "(a p) c -> p a c", p=128),
                         waits=[("PE", wo_guard.get(o % 2, 0))])
            for th in range(TC // 512):
                ps = PS[0 + (th % 2)]
                se_guard = kb.c("SE")
                for k in range(KD):
                    mm(ps[:, 0:512],
                       wbuf[:, k * 128:(k + 1) * 128],
                       yfull[:, k * TC + th * 512:k * TC + (th + 1) * 512],
                       start=(k == 0), stop=(k == KD - 1),
                       waits=([("LD", ldy), ("GD", ldw),
                               ("SE", max(0, se_guard - SEOFF))]
                              if k == 0 else []),
                       inc="PE")
                act(sst[:, th * 512:(th + 1) * 512], ps[:, 0:512], AF.Copy, 0.0, 1.0,
                    waits=[("PE", kb.c("PE")), ("LD", kb.c("LD"))])
            kb.dma("sync", s_dram[o * 128:(o + 1) * 128, :], sst[:, 0:TC],
                   waits=[("SE", kb.c("SE"))])
            wo_guard[o % 2] = kb.c("PE")
    kb.barrier()
    if upto <= 6:
        raise _Trunc(_ctx)

    scrB = ExitStack()
    _ctx["stacks"].append(scrB)
    alloc_scr(scrB, "_b")
    ftb = scrB.enter_context(nc.sbuf_tensor("ftb", [128, TC], F32))
    fzb = scrB.enter_context(nc.sbuf_tensor("fzb", [128, TC], F32))

    kb.phase = "P7"
    with ExitStack() as p7:
        xqT = p7.enter_context(nc.sbuf_tensor("xqT", [128, KD * TC], BF16))
        wg0 = p7.enter_context(nc.sbuf_tensor("wg0", [128, KD * 128], BF16))
        wg1 = p7.enter_context(nc.sbuf_tensor("wg1", [128, KD * 128], BF16))
        wu0 = p7.enter_context(nc.sbuf_tensor("wu0", [128, KD * 128], BF16))
        wu1 = p7.enter_context(nc.sbuf_tensor("wu1", [128, KD * 128], BF16))
        ygb = p7.enter_context(nc.sbuf_tensor("ygb", [128, 512], F32))
        ugb0 = p7.enter_context(nc.sbuf_tensor("ugb0", [128, 512], F32))
        ugb1 = p7.enter_context(nc.sbuf_tensor("ugb1", [128, 512], F32))

        kb.dma("gpsimd", ftb[:], ft_dram.unsqueeze(0).broadcast_to([128, TC]))
        GD_FTB = kb.c("GD")
        for k in range(KD):
            kb.dma("sync", xqT[:, k * TC:(k + 1) * TC],
                   xnqT_dram[k * 128:(k + 1) * 128, 0:TC])
        LD_XQT = kb.c("LD")

        wg_guard = {}
        p7_ve = [0, 0]
        for g in range(GN):
            wgb = wg0 if g % 2 == 0 else wg1
            wub = wu0 if g % 2 == 0 else wu1
            j, lc = g // (GN // n_cores), (g % (GN // n_cores)) * 128
            ldg = kb.dma("gpsimd", wgb[:].rearrange("p (a c) -> p a c", c=128),
                         wgq_out[j, :, lc:lc + 128].rearrange("(a p) c -> p a c", p=128),
                         waits=[("CC", CC_W), ("PE", wg_guard.get(g % 2, 0))])
            ldu = kb.dma("gpsimd", wub[:].rearrange("p (a c) -> p a c", c=128),
                         wuq_out[j, :, lc:lc + 128].rearrange("(a p) c -> p a c", p=128))
            for H in range(TC // 512):
                psg = PS[0 + (H % 2)]
                psu = PS[2 + (H % 2)]
                psf = PS[4 + (H % 2)]
                for k in range(KD):
                    mm(psg[:, 0:512], wgb[:, k * 128:(k + 1) * 128],
                       xqT[:, k * TC + H * 512:k * TC + (H + 1) * 512],
                       start=(k == 0), stop=(k == KD - 1),
                       waits=([("GD", ldg), ("LD", LD_XQT), ("VE", p7_ve[-2])]
                              if k == 0 else []),
                       inc="PE" if k == KD - 1 else None)
                for k in range(KD):
                    mm(psu[:, 0:512], wub[:, k * 128:(k + 1) * 128],
                       xqT[:, k * TC + H * 512:k * TC + (H + 1) * 512],
                       start=(k == 0), stop=(k == KD - 1),
                       waits=[("GD", ldu)] if k == 0 else [],
                       inc="PE" if k == KD - 1 else None)
                pev = kb.c("PE")
                vtt(Alu.mult, ygb[:], psg[:, 0:512], ftb[:, H * 512:(H + 1) * 512],
                    waits=[("PE", pev), ("GD", GD_FTB)], inc=True)
                act(evb0[:, 0:512], ygb[:], AF.Silu, zero_c[:, 0:1], sc_t[:, 0:1],
                    waits=[("VE", kb.c("VE"))])
                vtt(Alu.mult, ygb[:], psu[:, 0:512], ftb[:, H * 512:(H + 1) * 512],
                    waits=[("SE", kb.c("SE"))])
                vtt(Alu.mult, ygb[:], ygb[:], evb0[:, 0:512], inc=True)
                p7_ve.append(kb.c("VE"))
                if DEBUG_TAPS and g == 0 and H == 0:
                    kb.dma("sync", tap_y[0:128, 0:512], ygb[:],
                           waits=[("VE", kb.c("VE"))])
                    kb.dma("sync", tap_g[0:128, 0:512], evb0[:, 0:512],
                           waits=[("VE", kb.c("VE"))])
                mm(psf[:, 0:512], h128_t[:], ygb[:], True, True,
                   waits=[("VE", kb.c("VE")), ("SE", kb.c("SE"))], inc="PE")
                ub = ugb0 if (g * 2 + H) % 2 == 0 else ugb1
                act(ub[:], psf[:, 0:512], AF.Copy, 0.0, 1.0,
                    waits=[("PE", kb.c("PE")), ("SD", max(0, kb.c("SD") - 16))])
                kb.dma("scalar", u_dram[g, :, H * 512:(H + 1) * 512], ub[:],
                       waits=[("SE", kb.c("SE"))])
            wg_guard[g % 2] = kb.c("PE")
    kb.barrier()
    if upto <= 7:
        raise _Trunc(_ctx)

    kb.phase = "P8"
    with ExitStack() as p8:
        zT = p8.enter_context(nc.sbuf_tensor("zT", [128, IFFN], F32))
        zqT = p8.enter_context(nc.sbuf_tensor("zqT", [128, KI * 512], BF16))
        us0 = p8.enter_context(nc.sbuf_tensor("us0", [64, 8 * 128], F32))
        us1 = p8.enter_context(nc.sbuf_tensor("us1", [64, 8 * 128], F32))
        wd0 = p8.enter_context(nc.sbuf_tensor("wd0", [128, KI * 128], BF16))
        sgb = p8.enter_context(nc.sbuf_tensor("sgb", [128, 512], F32))
        zqb = p8.enter_context(nc.sbuf_tensor("zqb", [128, 2048], BF16))

        wd_guard = {}
        zqT_guard = 0
        for H in range(TC // 512):
            for tc4 in range(4):
                tch = H * 4 + tc4
                psz = PS[0 + (tc4 % 2)]
                for i1g in range(16):
                    se_guard = kb.c("SE")
                    ub = us0 if i1g % 2 == 0 else us1
                    ld = kb.dma("sync", ub[:].rearrange("p (a c) -> p a c", c=128),
                                u_dram[0:GN, i1g * 8:(i1g + 1) * 8,
                                       tch * 128:(tch + 1) * 128],
                                waits=[("PE", max(0, kb.c("PE") - 8))])
                    for i1l in range(8):
                        mm(psz[:, i1l * 64:(i1l + 1) * 64],
                           ub[:, i1l * 128:(i1l + 1) * 128], h64_t[:],
                           True, True,
                           waits=([("LD", ld), ("SE", se_guard)]
                                  if i1l == 0 else []),
                           inc="PE")
                    act(zT[:, i1g * 512:(i1g + 1) * 512], psz[:, 0:512],
                        AF.Identity, zero_c[:, 0:1], sc_t[:, 7:8],
                        waits=[("PE", kb.c("PE"))])
                SE_Z = kb.c("SE")
                kb.emit("vector", lambda e: e.reduce_max(
                    row1[:, 2:3], zT[:], AX.X, apply_absolute_value=True),
                    waits=[("SE", SE_Z)])
                vts(row1[:, 3:4], row1[:, 2:3], EPS_Q, None, Alu.max)
                vrecip(row1[:, 4:5], row1[:, 3:4])
                vts(row1[:, 5:6], row1[:, 4:5], 128.0, None, Alu.mult)
                vts(row2[:, tc4:tc4 + 1], row1[:, 3:4], 0.0078125, None, Alu.mult,
                    inc=True, waits=[("PE", kb.c("PE"))])
                for pc in range(IFFN // 2048):
                    sl = slice(pc * 2048, (pc + 1) * 2048)
                    act(SCR["TAs"][:], zT[:, sl], AF.Identity, zero_c[:, 0:1], row1[:, 5:6],
                        waits=[("VE", kb.c("VE"))])
                    emit_round(SCR["TAs"][:], SCR["QIs"][:], SCR["TBs"][:], SCR["Xs"][:],
                               waits=[("SE", kb.c("SE"))])
                    vts(zqb[:], SCR["TAs"][:], 127.0, None, Alu.min, inc=True,
                        waits=[("PE", kb.c("PE"))])
                    vev = kb.c("VE")
                    for i in range(16):
                        ci = pc * 16 + i
                        pst = PS[4 + (i % 2)]
                        tr(pst[:].bitcast(BF16)[:, 0:128],
                           zqb[:, i * 128:(i + 1) * 128], bf=True,
                           waits=[("VE", vev), ("SE", max(0, kb.c("SE") - 1)),
                                  ("PE", zqT_guard)])
                        act(zqT[:, ci * 512 + tc4 * 128:ci * 512 + (tc4 + 1) * 128],
                            pst[:].bitcast(BF16)[:, 0:128], AF.Copy, 0.0, 1.0,
                            waits=[("PE", kb.c("PE"))])
            SE_ZQT = kb.c("SE")
            tr(PS[6][0:4, 0:128], row2[:, 0:4], waits=[("VE", kb.c("VE"))])
            vcopy(evb1[0:4, 0:128], PS[6][0:4, 0:128], waits=[("PE", kb.c("PE"))],
                  inc=True)
            kb.dma("sync",
                   fz_dram.rearrange("(a b) -> a b", b=128)[H * 4:(H + 1) * 4, :],
                   evb1[0:4, 0:128], waits=[("VE", kb.c("VE"))])
            kb.dma("gpsimd", fzb[:, H * 512:(H + 1) * 512],
                   fz_dram[H * 512:(H + 1) * 512].unsqueeze(0).broadcast_to([128, 512]),
                   waits=[("LD", kb.c("LD"))])
            GD_FZB = kb.c("GD")

            p8_ve = [0, 0] if H == 0 else p8_ve
            for o in range(NO):
                wdb = wd0
                j, lo = (o * 128) // OSL, (o * 128) % OSL
                ldw = kb.dma("gpsimd", wdb[:].rearrange("p (a c) -> p a c", c=128),
                             wdq_out[j, :, lo:lo + 128].rearrange(
                                 "(a p) c -> p a c", p=128),
                             waits=[("PE", wd_guard.get(0, 0))])
                psd = PS[2 + (o % 2)]
                for k in range(KI):
                    mm(psd[:, 0:512], wdb[:, k * 128:(k + 1) * 128],
                       zqT[:, k * 512:(k + 1) * 512],
                       start=(k == 0), stop=(k == KI - 1),
                       waits=([("GD", ldw), ("SE", SE_ZQT),
                               ("VE", p8_ve[-2])] if k == 0 else []),
                       inc="PE" if k == KI - 1 else None)
                wd_guard[0] = kb.c("PE")
                pev = kb.c("PE")
                vtt(Alu.mult, sgb[:], psd[:, 0:512], fzb[:, H * 512:(H + 1) * 512],
                    waits=[("PE", pev), ("GD", GD_FZB), ("LD", kb.c("LD"))],
                    inc=True)
                act(evb0[:, 0:512], sgb[:], AF.Silu, zero_c[:, 0:1], sc_t[:, 2:3],
                    waits=[("VE", kb.c("VE"))])
                ld = kb.dma("sync", evb1[:, 0:512],
                            s_dram[o * 128:(o + 1) * 128, H * 512:(H + 1) * 512],
                            waits=[("SE", kb.c("SE")), ("VE", kb.c("VE"))])
                vtt(Alu.mult, sgb[:], evb0[:, 0:512], evb1[:, 0:512],
                    waits=[("LD", ld), ("SE", kb.c("SE"))], inc=True)
                p8_ve.append(kb.c("VE"))
                kb.dma("sync", s_dram[o * 128:(o + 1) * 128, H * 512:(H + 1) * 512],
                       sgb[:], waits=[("VE", kb.c("VE"))])
            zqT_guard = kb.c("PE")
    kb.barrier()
    if upto <= 8:
        raise _Trunc(_ctx)

    _res_id = [0]

    def emit_residual(sg_src, x_src, dst_tok, per_chunk=None):
        _res_id[0] += 1
        res_ve = [0, 0]
        with ExitStack() as rp:
            xacc = rp.enter_context(nc.sbuf_tensor(f"xacc_r{_res_id[0]}", [128, D], F32))
            sgs = rp.enter_context(nc.sbuf_tensor(f"sgs_r{_res_id[0]}", [128, NO * 128], F32))
            for tch in range(TCH):
                ldx = kb.dma("sync", xacc[:], x_src[tch * 128:(tch + 1) * 128, :],
                             waits=[("VE", kb.c("VE")), ("OD", kb.c("OD")),
                                    ("SE", kb.c("SE"))])
                lds = kb.dma("sync", sgs[:].rearrange("p (a c) -> p a c", c=128),
                             sg_src[:, tch * 128:(tch + 1) * 128].rearrange(
                                 "(a p) c -> p a c", p=128),
                             waits=[("PE", kb.c("PE"))])
                for o in range(NO):
                    pst = PS[4 + (o % 2)]
                    tr(pst[:, 0:128], sgs[:, o * 128:(o + 1) * 128],
                       waits=[("LD", lds), ("VE", res_ve[-2])])
                    kb.emit("vector", lambda e, o=o, pst=pst: e.tensor_add(
                        xacc[:, o * 128:(o + 1) * 128],
                        xacc[:, o * 128:(o + 1) * 128], pst[0:128, 0:128]),
                        waits=[("PE", kb.c("PE")), ("LD", ldx)], inc="VE")
                    res_ve.append(kb.c("VE"))
                if per_chunk is not None:
                    per_chunk(tch, xacc)
                kb.dma("sync", dst_tok[tch * 128:(tch + 1) * 128, :], xacc[:],
                       waits=[("VE", kb.c("VE")), ("SE", kb.c("SE"))],
                       sem="OD" if dst_tok is out else "LD")

    kb.phase = "P8b"
    with ExitStack() as p8b:
        xn1c = p8b.enter_context(nc.sbuf_tensor("xn1c", [128, D], F32))
        aqc = p8b.enter_context(nc.sbuf_tensor("aqc", [128, D], BF16))
        asm2 = p8b.enter_context(nc.sbuf_tensor("asm2", [128, 128], F32))
        asb2 = p8b.enter_context(nc.sbuf_tensor("asb2", [128, 128], BF16))

        def l1_prep(tch, xacc):
            for sg in range(4):
                kb.emit("vector", lambda e, x=xacc, sg=sg: e.bn_stats(
                    bst[:, sg * 6:(sg + 1) * 6], x[:, sg * 512:(sg + 1) * 512]),
                    waits=([("VE", kb.c("VE")), ("SE", kb.c("SE"))] if sg == 0 else []))
            kb.emit("vector", lambda e: e.bn_aggr(mv_t[:], bst[:]), inc="VE")
            act(row1[:, 0:1], mv_t[:, 1:2], AF.Sqrt, C(EPS_LN), 1.0,
                waits=[("VE", kb.c("VE"))])
            vrecip(row1[:, 1:2], row1[:, 0:1], waits=[("SE", kb.c("SE"))])
            kb.emit("vector", lambda e, x=xacc: e.scalar_tensor_tensor(
                xn1c[:].rearrange("p (a b) -> p a b", b=D),
                x[:].rearrange("p (a b) -> p a b", b=D),
                mv_t[:, 0:1],
                row1[:, 1:2].unsqueeze(2).broadcast_to([128, 1, D]),
                Alu.subtract, Alu.mult),
                waits=[("PE", kb.c("PE")), ("SE", kb.c("SE"))], inc="VE")
            mxfp4_chunk(xn1c[:], aqc[:],
                        waits=[("VE", kb.c("VE")), ("PE", kb.c("PE"))])
            for i in range(KD):
                pst = PS[6 + (i % 2)]
                tr(pst[:, 0:128], xn1c[:, i * 128:(i + 1) * 128],
                   waits=[("VE", kb.c("VE")), ("LD", kb.c("LD")),
                          ("SE", kb.c("SE"))])
                act(asm2[:], pst[:, 0:128], AF.Copy, 0.0, 1.0,
                    waits=[("PE", kb.c("PE")), ("LD", kb.c("LD"))])
                kb.dma("sync",
                       xn1T_dram[i * 128:(i + 1) * 128, tch * 128:(tch + 1) * 128],
                       asm2[:], waits=[("SE", kb.c("SE"))])
                tr(pst[:].bitcast(BF16)[:, 0:128], aqc[:, i * 128:(i + 1) * 128],
                   bf=True, waits=[("LD", kb.c("LD")), ("SE", kb.c("SE"))])
                act(asb2[:], pst[:].bitcast(BF16)[:, 0:128], AF.Copy, 0.0, 1.0,
                    waits=[("PE", kb.c("PE")), ("LD", kb.c("LD"))])
                kb.dma("sync",
                       aqT_dram[i * 128:(i + 1) * 128, tch * 128:(tch + 1) * 128],
                       asb2[:], waits=[("SE", kb.c("SE"))])

        emit_residual(s_dram, x_in, xnew_dram, per_chunk=l1_prep)
    kb.barrier()
    if upto <= 9:
        raise _Trunc(_ctx)

    kb.phase = "P9"
    with ExitStack() as p9:
        wv0 = p9.enter_context(nc.sbuf_tensor("wv0", [128, KD * 128], F32))
        l1w0 = p9.enter_context(nc.sbuf_tensor("l1w0", [128, KD * 128], BF16))
        l1w1 = p9.enter_context(nc.sbuf_tensor("l1w1", [128, KD * 128], BF16))
        mvf = p9.enter_context(nc.sbuf_tensor("mvf", [128, KD * 512], F32))
        aqf = p9.enter_context(nc.sbuf_tensor("aqf", [128, KD * TC], BF16))
        v1t = p9.enter_context(nc.sbuf_tensor("v1t", [128, TC], F32))
        g1b = p9.enter_context(nc.sbuf_tensor("g1b", [128, 512], F32))

        ldwv = kb.dma("gpsimd", wv0[:].rearrange("p (a c) -> p a c", c=128),
                      l1wvT[:, :].rearrange("(a p) c -> p a c", p=128))
        for k in range(KD):
            ldaq = kb.dma("sync", aqf[:, k * TC:(k + 1) * TC],
                          aqT_dram[k * 128:(k + 1) * 128, :])
        for th in range(TC // 512):
            ps = PS[0 + (th % 2)]
            se_guard = kb.c("SE")
            pe_g = kb.c("PE")
            for k in range(KD):
                ld = kb.dma("sync", mvf[:, k * 512:(k + 1) * 512],
                            xn1T_dram[k * 128:(k + 1) * 128, th * 512:(th + 1) * 512],
                            waits=[("PE", pe_g)] if k == 0 else ())
                mm(ps[:, 0:512], wv0[:, k * 128:(k + 1) * 128],
                   mvf[:, k * 512:(k + 1) * 512],
                   start=(k == 0), stop=(k == KD - 1),
                   waits=([("LD", ld), ("GD", ldwv), ("SE", max(0, se_guard - SEOFF))]
                          if k == 0 else [("LD", ld)]),
                   inc="PE")
            act(v1t[:, th * 512:(th + 1) * 512], ps[:, 0:512], AF.Identity,
                l1bv_t[:, 0:1], ones_c[:, 0:1], waits=[("PE", kb.c("PE"))])
        SE_V1 = kb.c("SE")
        if upto <= 10:
            raise _Trunc(_ctx)

        l1_guard = {}
        for o in range(NO):
            wb = l1w0 if o % 2 == 0 else l1w1
            j, lo = (o * 128) // OSL, (o * 128) % OSL
            ldw = kb.dma("gpsimd", wb[:].rearrange("p (a c) -> p a c", c=128),
                         l1q_out[j, :, lo:lo + 128].rearrange("(a p) c -> p a c", p=128),
                         waits=[("PE", l1_guard.get(o % 2, 0))])
            for th in range(TC // 512):
                ps = PS[2 + (th % 2)]
                se_guard = kb.c("SE")
                for k in range(KD):
                    mm(ps[:, 0:512], wb[:, k * 128:(k + 1) * 128],
                       aqf[:, k * TC + th * 512:k * TC + (th + 1) * 512],
                       start=(k == 0), stop=(k == KD - 1),
                       waits=([("LD", ldaq), ("GD", ldw),
                               ("SE", max(0, se_guard - SEOFF))]
                              if k == 0 else []),
                       inc="PE")
                act(g1b[:], ps[:, 0:512], AF.Silu, l1bias_t[:, o:o + 1],
                    ones_c[:, 0:1],
                    waits=[("PE", kb.c("PE")), ("SE", SE_V1),
                           ("VE", kb.c("VE")), ("LD", kb.c("LD"))])
                vtt(Alu.mult, g1b[:], g1b[:], v1t[:, th * 512:(th + 1) * 512],
                    waits=[("SE", kb.c("SE")), ("LD", kb.c("LD"))], inc=True)
                kb.dma("sync",
                       s_dram[o * 128:(o + 1) * 128, th * 512:(th + 1) * 512],
                       g1b[:], waits=[("VE", kb.c("VE"))])
            l1_guard[o % 2] = kb.c("PE")
    kb.barrier()
    if upto <= 11:
        raise _Trunc(_ctx)
    if upto <= 12:
        dbg_res = dint("dbg_res", [TC, D])
        emit_residual(s_dram, xnew_dram, dbg_res)
        raise _Trunc(_ctx)
    if upto <= 13:
        emit_residual(s_dram, xnew_dram, out)
        raise _Trunc(_ctx)

    kb.phase = "RES"
    emit_residual(s_dram, xnew_dram, out)
    if DEBUG_DUMP:
        kb.wait("sync", "OD", kb.c("OD"))
        for nm, t in (("d_xnT", xnT_dram), ("d_ft", ft_dram),
                      ("d_amine", a_mine_dram), ("d_ag1", ag1_out),
                      ("d_ysel", y_sel_dram), ("d_u", u_dram),
                      ("d_fz", fz_dram), ("d_s", s_dram),
                      ("d_xnew", xnew_dram), ("d_xn1T", xn1T_dram),
                      ("d_xnqT", xnqT_dram)):
            sh = list(t.tensor.shape) if hasattr(t, "tensor") else None
            dt_ = t.tensor.dtype
            dbg = nc.dram_tensor(nm, sh, dt_, kind="ExternalOutput").ap()
            if len(sh) == 1:
                kb.dma("sync", dbg[:], t[:], sem="OD")
            elif len(sh) == 2:
                kb.dma("sync", dbg[:, :], t[:, :], sem="OD")
            else:
                kb.dma("sync", dbg[:, :, :], t[:, :, :], sem="OD")
    kb.wait("sync", "OD", kb.c("OD"))
    scrB.close()

    return _finish(nc, kb, top)


def _hadamard(n):
    h = np.array([[1.0]], dtype=np.float64)
    while h.shape[0] < n:
        h = np.block([[h, h], [h, -h]])
    return h


def _prep_inputs(inputs, n_cores, TC):
    T = n_cores * TC
    HPC = NH // n_cores
    NA = (RQ + RK + RV) * HPC
    CSL = IFFN // n_cores
    OSL = D // n_cores

    x = np.ascontiguousarray(np.asarray(inputs["L0_attn_x"], dtype=np.float32)
                             .reshape(T, D))
    W_q = np.asarray(inputs["L0_attn_W_q"], dtype=np.float32)
    W_kv = np.asarray(inputs["L0_attn_W_kv"], dtype=np.float32)
    W_o = np.asarray(inputs["L0_attn_W_o"], dtype=np.float32)
    w_gate = np.asarray(inputs["L0_ffn_w_gate"], dtype=np.float32)
    w_up = np.asarray(inputs["L0_ffn_w_up"], dtype=np.float32)
    w_down = np.asarray(inputs["L0_ffn_w_down"], dtype=np.float32)
    l1w = np.asarray(inputs["L1_ffn_weight"], dtype=np.float32)
    l1b = np.asarray(inputs["L1_ffn_bias"], dtype=np.float32)
    l1wkv = np.asarray(inputs["L1_attn_Wkv"], dtype=np.float32)
    l1bkv = np.asarray(inputs["L1_attn_bkv"], dtype=np.float32)

    acols = []
    for j in range(n_cores):
        for r in range(RQ):
            for hl in range(HPC):
                acols.append(("q", r * 144 + j * HPC + hl))
        for r in range(RK + RV):
            for hl in range(HPC):
                acols.append(("kv", r * 144 + j * HPC + hl))
    waT_np = np.empty((D, NA * n_cores), dtype=np.float32)
    for i, (kind, ri) in enumerate(acols):
        waT_np[:, i] = (W_q if kind == "q" else W_kv)[ri, :]
    bcols = []
    for r in range(RQ):
        bcols.extend(("q", r * 144 + 16 + d) for d in range(DH))
    for r in range(RK + RV):
        bcols.extend(("kv", r * 144 + 16 + d) for d in range(DH))
    wbT_np = np.empty((D, NB * DH), dtype=np.float32)
    for i, (kind, ri) in enumerate(bcols):
        wbT_np[:, i] = (W_q if kind == "q" else W_kv)[ri, :]

    woT_np = np.ascontiguousarray(W_o.T)
    perm = np.array([(cp % 128) * 64 + cp // 128 for cp in range(IFFN)])
    wgT_pm = np.ascontiguousarray(w_gate.T[:, perm])
    wuT_pm = np.ascontiguousarray(w_up.T[:, perm])
    wdT_np = np.ascontiguousarray(w_down.T)
    l1wvT_np = np.ascontiguousarray(l1wkv[DH:2 * DH, :].T)
    l1bv_np = np.ascontiguousarray(l1bkv[DH:2 * DH].reshape(DH, 1))

    identf_np = np.eye(128, dtype=np.float32)
    identb_np = np.eye(128, dtype=np.float32).astype(ml_dtypes.bfloat16)
    h128_np = _hadamard(128).astype(np.float32)
    h64_np = _hadamard(64).astype(np.float32)
    masks_np = np.zeros((128, 4 * 512), dtype=np.float32)
    for jj in range(4):
        for p in range(128):
            cstart = jj * 128 + p
            if cstart < 512:
                masks_np[p, jj * 512 + cstart:(jj + 1) * 512] = 1.0

    inv_freq = (1.0 / 10000.0) ** (np.arange(0, DH, 2, dtype=np.float64) / DH)

    in_maps = []
    for c in range(n_cores):
        t0 = c * TC
        tloc = np.arange(t0, t0 + TC, dtype=np.float64)
        fr = tloc[None, :] * inv_freq[:, None]
        cosd, sind = np.cos(fr), np.sin(fr)
        ch = cosd.astype(np.float32)
        cl = (cosd - ch.astype(np.float64)).astype(np.float32)
        sh = sind.astype(np.float32)
        sl = (sind - sh.astype(np.float64)).astype(np.float32)
        ctab_np = np.concatenate([np.tile(ch, (2, 1)), np.tile(cl, (2, 1))], axis=1)
        stab_np = np.concatenate([np.tile(sh, (2, 1)), np.tile(sl, (2, 1))], axis=1)
        im = {
            "x_slice": np.ascontiguousarray(x[t0:t0 + TC]),
            "waT": waT_np, "wbT": wbT_np, "woT": woT_np,
            "wgTs": np.ascontiguousarray(wgT_pm[:, c * CSL:(c + 1) * CSL]),
            "wuTs": np.ascontiguousarray(wuT_pm[:, c * CSL:(c + 1) * CSL]),
            "wdTs": np.ascontiguousarray(wdT_np[:, c * OSL:(c + 1) * OSL]),
            "l1w_s": np.ascontiguousarray(l1w[c * OSL:(c + 1) * OSL, :]),
            "l1wvT": l1wvT_np, "l1bv": l1bv_np,
            "l1bias": np.ascontiguousarray(l1b),
            "identf": identf_np, "identb": identb_np,
            "h128": h128_np, "h64": h64_np, "masks": masks_np,
            "ctab": np.ascontiguousarray(ctab_np),
            "stab": np.ascontiguousarray(stab_np),
            "rank_id": np.array([[c]], dtype=np.int32),
        }
        in_maps.append(im)
    return in_maps


def kernel(**inputs):
    n_cores = 8
    TC = 8192 // n_cores
    in_maps = _prep_inputs(inputs, n_cores, TC)
    nc = build_program(n_cores, TC)
    res = run_bass_kernel_spmd(nc, in_maps, core_ids=list(range(n_cores)))
    outs = [np.asarray(r["out"], dtype=np.float32) for r in res.results]
    return np.concatenate(outs, axis=0).reshape(1, 8192, D)
